# version 10
# baseline (speedup 1.0000x reference)
import numpy as np
import concourse.bass as bass
import concourse.mybir as mybir
from concourse.bass_utils import run_bass_kernel_spmd

F32 = mybir.dt.float32
BF16 = mybir.dt.bfloat16
I32 = mybir.dt.int32
AF = mybir.ActivationFunctionType
ALU = mybir.AluOpType
AXX = mybir.AxisListType.X

S, D, H = 2048, 1024, 3072
NSEQ = 2
NT = S // 128
DC = D // 128
HT = H // 128
NG = 4
GT = S // NG
TPG = GT // 128
NQG = 4
QPG = NT // NQG

PI = float(np.float32(np.pi))
TWO_PI = float(np.float32(2 * np.pi))
INV_2PI = float(np.float32(1.0 / (2 * np.pi)))
CLAMP = PI * 0.9999995
LN1E4 = float(np.log(10000.0))

ENGS = ("sync", "scalar", "vector", "tensor", "gpsimd")

DBG_STAGE = "full"
DBG_NOROPE = False


class Em:
    def __init__(self):
        self.streams = {e: [] for e in ENGS}
        self.cnt = dict(pe=0, ve=0, ac=0, gp=0, dW=0, dS=0, dG=0, dP=0)
        self.watermark = {e: {s: 0 for s in self.cnt} for e in ENGS}

    def op(self, eng, build, waits=(), inc=None, n=1):
        w = {}
        for t in waits:
            if t is None:
                continue
            s, v = t
            if v > self.watermark[eng][s]:
                w[s] = max(w.get(s, 0), v)
        for s, v in w.items():
            self.watermark[eng][s] = v
        tok = None
        if inc is not None:
            self.cnt[inc] += n
            tok = (inc, self.cnt[inc])
        self.streams[eng].append((tuple(w.items()), build, tok))
        return tok

    def dma(self, eng, out_ap, in_ap, waits=(), transpose=False, sem=None):
        if sem is None:
            sem = {"sync": "dW", "scalar": "dS", "gpsimd": "dG"}[eng]
        if transpose:
            b = lambda e, o=out_ap, i=in_ap: e.dma_start(out=o, in_=i, transpose=True)
        else:
            b = lambda e, o=out_ap, i=in_ap: e.dma_start(out=o, in_=i)
        return self.op(eng, b, waits=waits, inc=sem, n=16)


def build_program():
    nc = bass.Bass("TRN2", target_bir_lowering=False)

    x_in = nc.dram_tensor("x_in", [NSEQ, S, D], F32, kind="ExternalInput").ap()
    wq = nc.dram_tensor("wq", [D, D], F32, kind="ExternalInput").ap()
    wk = nc.dram_tensor("wk", [D, D], F32, kind="ExternalInput").ap()
    wv = nc.dram_tensor("wv", [D, D], F32, kind="ExternalInput").ap()
    bq = nc.dram_tensor("bq", [1, D], F32, kind="ExternalInput").ap()
    bk = nc.dram_tensor("bk", [1, D], F32, kind="ExternalInput").ap()
    bv = nc.dram_tensor("bv", [1, D], F32, kind="ExternalInput").ap()
    wg = nc.dram_tensor("wg", [H, D], I32, kind="ExternalInput").ap()
    wu = nc.dram_tensor("wu", [H, D], I32, kind="ExternalInput").ap()
    wd = nc.dram_tensor("wd", [D, H], I32, kind="ExternalInput").ap()
    x_out = nc.dram_tensor("x_out", [NSEQ, S, D], F32, kind="ExternalOutput").ap()

    dbg1 = "ExternalOutput" if DBG_STAGE == "ffn1" else "Internal"
    wg_s = nc.dram_tensor("wg_s", [H, D], BF16, kind=dbg1).ap()
    wu_s = nc.dram_tensor("wu_s", [H, D], BF16, kind=dbg1).ap()
    wd_s = nc.dram_tensor("wd_s", [D, H], BF16, kind=dbg1).ap()
    dbg2 = "ExternalOutput" if DBG_STAGE == "qkv" else "Internal"
    wqp_s = nc.dram_tensor("wqp_s", [D, D], BF16, kind=dbg2).ap()
    wkp_s = nc.dram_tensor("wkp_s", [D, D], BF16, kind=dbg2).ap()
    wv_s = nc.dram_tensor("wv_s", [D, D], BF16).ap()
    tbl_s = nc.dram_tensor("tbl_s", [2, S, 512], F32, kind=dbg2).ap()
    b_s = nc.dram_tensor("b_s", [1, 3 * D], BF16).ap()
    wgT_s = nc.dram_tensor("wgT_s", [D, H], BF16).ap()
    wuT_s = nc.dram_tensor("wuT_s", [D, H], BF16).ap()
    wdT_s = nc.dram_tensor("wdT_s", [H, D], BF16).ap()
    wqT_s = nc.dram_tensor("wqT_s", [D, D], BF16).ap()
    wkT_s = nc.dram_tensor("wkT_s", [D, D], BF16).ap()
    wvT_s = nc.dram_tensor("wvT_s", [D, D], BF16).ap()
    dbg = DBG_STAGE == "qkv"
    xn_s = nc.dram_tensor("xn_s", [S, D], BF16,
                          kind="ExternalOutput" if DBG_STAGE == "ffn1" else "Internal").ap()
    h_dbg = (nc.dram_tensor("h_dbg", [128, HT, 512], BF16, kind="ExternalOutput").ap()
             if DBG_STAGE == "ffn1" else None)
    q_s = nc.dram_tensor("q_s", [S, D], BF16,
                         kind="ExternalOutput" if dbg else "Internal").ap()
    k_s = nc.dram_tensor("k_s", [S, D], BF16,
                         kind="ExternalOutput" if dbg else "Internal").ap()
    p_s = nc.dram_tensor("p_s", [GT, S], BF16).ap()
    v_dbg = (nc.dram_tensor("v_dbg", [S, D], BF16, kind="ExternalOutput").ap()
             if dbg else None)
    wqt_dbg = (nc.dram_tensor("wqt_dbg", [128, DC, D], BF16, kind="ExternalOutput").ap()
               if dbg else None)
    xnt_dbg = (nc.dram_tensor("xnt_dbg", [128, DC, S], BF16, kind="ExternalOutput").ap()
               if dbg else None)

    BASE, TOP = 16512, 229344

    def at(name, shape, dtype, offset):
        return nc.alloc_sbuf_tensor_at(name, shape, dtype, offset=offset).ap()

    o = BASE
    x_res = at("x_res", [128, NT, D], F32, o); o += NT * D * 4
    OFF_XNT = o; o += DC * S * 2
    xnt = at("xnt", [128, DC, S], BF16, OFF_XNT)
    xnt_f = at("xnt_f", [128, 2, DC, 512], BF16, OFF_XNT)
    hT_a = at("hT_a", [128, 16, 512], BF16, OFF_XNT + 16384)
    OFF_A = o; o += HT * D * 2
    wdt = at("wdt", [128, HT, D], BF16, OFF_A)
    v_res = at("v_res", [128, NT, D], BF16, OFF_A)
    qtb = at("qtb", [128, DC, 512], BF16, OFF_A + NT * D * 2)
    bias_qkv = at("bias_qkv", [1, 3 * D], BF16, OFF_A + NT * D * 2 + 8192)
    OFF_WIN = o; o += 32768
    wg_win = at("wg_win", [128, 2, DC, 512], BF16, OFF_WIN)
    rw_stg = at("rw_stg", [128, 2, H], BF16, OFF_WIN)
    rw_qtb = at("rw_qtb", [128, 2, D], BF16, OFF_A + NT * D * 2)
    stg8_i = at("stg8_i", [128, 8, 512], I32, OFF_WIN + 12288)
    stg8_f = at("stg8_f", [128, 8, 512], F32, OFF_WIN + 12288)
    stg4_b = at("stg4_b", [128, 4, 512], BF16, OFF_WIN + 28672)
    wu_win = at("wu_win", [128, 2, DC, 512], BF16, OFF_WIN + 16384)
    wqt = at("wqt", [128, DC, D], BF16, OFF_WIN)
    wkt = at("wkt", [128, DC, D], BF16, OFF_WIN + 16384)
    wvt = at("wvt", [128, DC, D], BF16, OFF_WIN)
    pT = at("pT", [128, 2, 16, 512], BF16, OFF_WIN)
    OFF_TBL = o; o += 8192
    tbl = at("tbl", [128, 2, 2, 512], F32, OFF_TBL)
    hT_b = at("hT_b", [128, 8, 512], BF16, OFF_TBL)
    invf_bc = at("invf_bc", [128, 512], F32, OFF_TBL)
    invf = at("invf", [1, 512], F32, OFF_TBL + 2048)
    jrow_f = at("jrow_f", [1, 512], F32, OFF_TBL + 4096)
    OFF_S1 = o; o += 4096
    p_sb = at("p_sb", [128, S], BF16, OFF_S1)
    OFF_S2 = o; o += 4096
    silu_sb = at("silu_sb", [128, 2, 512], BF16, OFF_S2)
    qtb_b = at("qtb_b", [128, DC, 512], BF16, OFF_S2)
    rtmp1 = at("rtmp1", [128, 512], F32, OFF_S2)
    rtmp2 = at("rtmp2", [128, 512], F32, OFF_S2 + 2048)
    OFF_QROT = o; o += 4096
    qrot = at("qrot", [128, D], BF16, OFF_QROT)
    krot = at("krot", [128, D], BF16, OFF_QROT + 2048)
    stage_b = at("stage_b", [128, 2, 512], BF16, OFF_QROT)
    tg_arg = at("tg_arg", [128, 512], F32, OFF_QROT + 2048)
    OFF_STG = o; o += 4096
    stage_i = at("stage_i", [128, 2, 512], I32, OFF_STG)
    stage_f = at("stage_f", [128, 2, 512], F32, OFF_STG)
    tg_ang = at("tg_ang", [128, 512], F32, OFF_STG)
    tg_i32 = at("tg_i32", [128, 512], I32, OFF_STG + 2048)
    brow_f32 = at("brow_f32", [1, D], F32, OFF_STG)
    OFF_SQ = o; o += 4096
    sq_sb = at("sq_sb", [128, D], BF16, OFF_SQ)
    jrow_i = at("jrow_i", [1, 512], I32, OFF_SQ + 2048)
    brow_bf = at("brow_bf", [1, D], BF16, o); o += 2048
    xn_sb = at("xn_sb", [128, 2, D], BF16, OFF_SQ + 2048)
    ones_bf = at("ones_bf", [1, 128], BF16, o); o += 288
    ones_f32 = at("ones_f32", [1, 128], F32, o); o += 512
    pos_i = at("pos_i", [128, NT], I32, o); o += 64
    pos_f = at("pos_f", [128, NT], F32, o); o += 64
    sums2 = at("sums2", [128, 2, 2], F32, o); o += 32
    mu = at("mu", [128, 2], F32, o); o += 32
    mu2 = at("mu2", [128, 2], F32, o); o += 32
    var_ = at("var_", [128, 2], F32, o); o += 32
    lnv = at("lnv", [128, 2], F32, o); o += 32
    rstd = at("rstd", [128, 2], F32, o); o += 32
    negmur = at("negmur", [128, 2], F32, o); o += 32
    mx4 = at("mx4", [128, 2, 4], F32, o); o += 64
    rs4 = at("rs4", [128, 2, 4], F32, o); o += 64
    nm32 = at("nm32", [128, NT], F32, o); o += 64
    rowsum = at("rowsum", [128, NT], F32, o); o += 64
    recip = at("recip", [128, NT], F32, o); o += 64
    assert o <= TOP, f"SBUF overflow: {o} > {TOP}"

    ps = nc.alloc_psum_tensor("ps", [128, 4096], F32).ap()

    em = Em()
    E = em.op
    DMA = em.dma
    ctx = {}
    def TDMA(out_ap, in_ap, waits=()):
        return em.dma("sync", out_ap, in_ap, waits=waits, transpose=True)

    wgTv = wgT_s.rearrange("(c p) h -> p c h", p=128)
    wuTv = wuT_s.rearrange("(c p) h -> p c h", p=128)
    wdTv = wdT_s.rearrange("(i p) d -> p i d", p=128)
    wqTv = wqT_s.rearrange("(c p) e -> p c e", p=128)
    wkTv = wkT_s.rearrange("(c p) e -> p c e", p=128)
    wvTv = wvT_s.rearrange("(c p) e -> p c e", p=128)

    prep_state = {"st": {}, "ci": 0, "q": 0}
    pending_casts = []

    def prep_chunk(load_src, store_dst, scale):
        ci = prep_state["ci"]
        prep_state["ci"] += 1
        par8 = ci % 8
        par4 = ci % 4
        cols = load_src.shape[-1]
        is_int = load_src.dtype == I32
        lw = [("dP", 16 * (ci - 7))] if ci >= 8 else []
        ld = DMA("gpsimd", stg8_i[:, par8, 0:cols] if is_int else stg8_f[:, par8, 0:cols],
                 load_src, waits=lw)
        holder = {}

        def cast():
            srcap = stg8_i[:, par8, 0:cols] if is_int else stg8_f[:, par8, 0:cols]
            cv = E("scalar", lambda e, s=srcap, p=par4, c=cols, sc=scale:
                   e.activation(out=stg4_b[:, p, 0:c], in_=s, func=AF.Copy, scale=sc),
                   waits=[ld, prep_state["st"].get(par4)], inc="ac")
            st = DMA("scalar", store_dst, stg4_b[:, par4, 0:cols], waits=[cv],
                     sem="dP")
            prep_state["st"][par4] = st
            holder["tok"] = st
        pending_casts.append(cast)
        return holder

    def drain_casts(k=10 ** 9):
        n = 0
        while pending_casts and n < k:
            pending_casts.pop(0)()
            n += 1

    wd_hold = []
    def emit_wd_prep():
        for j in range(6):
            h = None
            for i in range(8):
                h = prep_chunk(wd[128 * i:128 * (i + 1), 512 * j:512 * (j + 1)],
                               wd_s[128 * i:128 * (i + 1), 512 * j:512 * (j + 1)],
                               1.0 / 127.0)
            wd_hold.append(h)

    gu_hold = {}
    def emit_gu_prep():
        for name, src, dst in (("g", wg, wg_s), ("u", wu, wu_s)):
            for w in range(6):
                h = None
                for i in range(4):
                    r = 512 * w + 128 * i
                    for j in range(2):
                        h = prep_chunk(src[r:r + 128, 512 * j:512 * (j + 1)],
                                       dst[r:r + 128, 512 * j:512 * (j + 1)],
                                       1.0 / 127.0)
                gu_hold[(name, w)] = h

    qkv_hold = {}
    def emit_qk_prep(name, src, dst):
        h = None
        for half in range(2):
            for i in range(4):
                for j in range(2):
                    h = prep_chunk(
                        src[half + 256 * i:half + 256 * i + 255:2,
                            512 * j:512 * (j + 1)],
                        dst[512 * half + 128 * i:512 * half + 128 * (i + 1),
                            512 * j:512 * (j + 1)],
                        1.0)
        qkv_hold[name] = h

    def emit_v_prep():
        h = None
        for i in range(8):
            for j in range(2):
                h = prep_chunk(wv[128 * i:128 * (i + 1), 512 * j:512 * (j + 1)],
                               wv_s[128 * i:128 * (i + 1), 512 * j:512 * (j + 1)],
                               1.0)
        qkv_hold["v"] = h

    bias_tok = [None]
    def emit_bias_prep():
        drain_casts()
        t = prep_state["st"].get(0)
        t2 = prep_state["st"].get(3)
        ctx["prep_stores"] = [prep_state["st"].get(i) for i in range(4)]
        wprev = tbl_done[0]
        for idx, (src, perm) in enumerate(((bq, True), (bk, True), (bv, False))):
            ld = DMA("gpsimd", brow_f32[:, :], src[:, :],
                     waits=[t, t2, wprev, ctx.get("ln1_done")])
            if perm:
                E("vector", lambda e: e.tensor_copy(
                    out=brow_bf[0:1, 0:512], in_=brow_f32[0:1, 0:D:2]),
                    waits=[ld, wprev], inc="ve")
                t = E("vector", lambda e: e.tensor_copy(
                    out=brow_bf[0:1, 512:1024], in_=brow_f32[0:1, 1:D:2]),
                    waits=[ld], inc="ve")
            else:
                t = E("vector", lambda e: e.tensor_copy(
                    out=brow_bf[0:1, :], in_=brow_f32[0:1, :]),
                    waits=[ld, wprev], inc="ve")
            t2 = t
            wprev = DMA("scalar", b_s[0:1, D * idx:D * (idx + 1)], brow_bf[0:1, :],
                        waits=[t])
        bias_tok[0] = wprev
        ctx["brow_done"] = wprev

    PREP_TOTAL = 192
    rw_done = {}
    def rewrite_one(name, nat, dst, rows, cols, stage, extra, rwtok):
        n = cols // 128
        rds = []
        for c in range(n + 1):
            par = c % 2
            cc = min(c, n - 1)
            rd = TDMA(stage[:, par, 0:rows], nat[:, 128 * cc:128 * (cc + 1)],
                      waits=[rwtok.get(par)] + extra)
            rds.append(rd)
            if c >= 1:
                wr = DMA("scalar", dst[128 * (c - 1):128 * c, :],
                         stage[:, (c - 1) % 2, 0:rows], waits=[rds[c - 1], rd])
                rwtok[(c - 1) % 2] = wr
        rw_done[name] = wr
        return wr

    def emit_rewrites_gu():
        rwtok = {}
        rewrite_one("wg", wg_s, wgT_s, H, D, rw_stg,
                    [gu_hold[("g", 5)].get("tok")], rwtok)
        rewrite_one("wu", wu_s, wuT_s, H, D, rw_stg,
                    [gu_hold[("u", 5)].get("tok")], rwtok)
        rw_done["gu"] = rw_done["wu"]

    def emit_rewrites_wd_qkv(x_done):
        rwtok = {}
        rewrite_one("wd", wd_s, wdT_s, D, H, rw_qtb,
                    [x_done, wd_hold[5].get("tok")], rwtok)
        rewrite_one("wq", wqp_s, wqT_s, D, D, rw_qtb,
                    [qkv_hold["q"].get("tok")], rwtok)
        rewrite_one("wk", wkp_s, wkT_s, D, D, rw_qtb,
                    [qkv_hold["k"].get("tok")], rwtok)
        last = rewrite_one("wv", wv_s, wvT_s, D, D, rw_qtb,
                           [qkv_hold["v"].get("tok")], rwtok)
        ctx["qtb_guard"] = last

    tbl_done = [None]
    def emit_tables(bank7_guard, xn_guard=None):
        iota1 = E("gpsimd", lambda e: e.iota(jrow_i[:, :], pattern=[[1, 512]], base=0,
                                             channel_multiplier=0),
                  waits=[xn_guard], inc="gp")
        iota2 = E("gpsimd", lambda e: e.iota(pos_i[:, :], pattern=[[128, NT]], base=0,
                                             channel_multiplier=1), inc="gp")
        onf = E("gpsimd", lambda e: e.memset(ones_f32[:, :], 1.0), inc="gp")
        jf = E("vector", lambda e: e.tensor_copy(out=jrow_f[:, :], in_=jrow_i[:, :]),
               waits=[iota1] + bank7_guard, inc="ve")
        pf = E("vector", lambda e: e.tensor_copy(out=pos_f[:, :], in_=pos_i[:, :]),
               waits=[iota2], inc="ve")
        inv = E("scalar", lambda e: e.activation(out=invf[:, :], in_=jrow_f[:, :],
                                                 func=AF.Exp, scale=-LN1E4 / 512.0),
                waits=[jf], inc="ac")
        bcm = E("tensor", lambda e: e.matmul(ps[:, 3584:4096], ones_f32[:, :],
                                             invf[:, :], start=True, stop=True),
                waits=[inv, onf] + bank7_guard, inc="pe")
        bcc = E("vector", lambda e: e.tensor_copy(out=invf_bc[:, :],
                                                  in_=ps[:, 3584:4096]),
                waits=[bcm], inc="ve")
        ctx["b7_dve"] = bcc
        prev = {}
        last = None
        for t in range(NT):
            a1 = E("vector", lambda e, t=t: e.tensor_scalar_mul(
                tg_ang[:, :], invf_bc[:, :], pos_f[:, t:t + 1]),
                waits=[bcc, pf, prev.get(1)], inc="ve")
            for which, src, shift in ((0, tg_arg, PI / 2), (1, tg_ang, 0.0)):
                if shift:
                    s1 = E("vector", lambda e: e.tensor_scalar_add(
                        tg_arg[:, :], tg_ang[:, :], PI / 2),
                        waits=[a1, prev.get(0)], inc="ve")
                else:
                    s1 = a1
                k1 = E("vector", lambda e, sr=src: e.tensor_scalar_mul(
                    tg_i32[:, :], sr[:, :], INV_2PI), waits=[s1], inc="ve")
                r1 = E("vector", lambda e, sr=src: e.scalar_tensor_tensor(
                    out=sr[:, :], in0=tg_i32[:, :], scalar=-TWO_PI, in1=sr[:, :],
                    op0=ALU.mult, op1=ALU.add), waits=[k1], inc="ve")
                c1 = E("vector", lambda e, sr=src: e.tensor_scalar(
                    out=sr[:, :], in0=sr[:, :], scalar1=CLAMP, scalar2=-CLAMP,
                    op0=ALU.min, op1=ALU.max), waits=[r1], inc="ve")
                sn = E("scalar", lambda e, sr=src: e.activation(
                    out=sr[:, :], in_=sr[:, :], func=AF.Sin), waits=[c1], inc="ac")
                dm = DMA("scalar", tbl_s[which, 128 * t:128 * (t + 1), :], src[:, :],
                         waits=[sn])
                prev[which] = dm
                last = dm
        tbl_done[0] = last
        return bcc

    def x_load(s, eng):
        toks = []
        xr = x_in[s].rearrange("(t p) d -> p t d", p=128)
        for i in range(4):
            toks.append(DMA(eng, x_res[:, 4 * i:4 * (i + 1), :],
                            xr[:, 4 * i:4 * i + 4, :]))
        return toks

    def ln_phase(s, xready, weave=None):
        wr = [None] * NT
        for t in range(NT):
            par = t % 2
            xw = list(xready[t])
            r1 = E("vector", lambda e, t=t, p=par: e.reduce_sum(
                out=sums2[:, p, 0:1], in_=x_res[:, t, 0:512], axis=AXX),
                waits=xw, inc="ve")
            r2 = E("vector", lambda e, t=t, p=par: e.reduce_sum(
                out=sums2[:, p, 1:2], in_=x_res[:, t, 512:1024], axis=AXX),
                waits=xw, inc="ve")
            sq = E("scalar", lambda e, t=t, p=par: e.activation(
                out=sq_sb[:, :], in_=x_res[:, t, :], func=AF.Square,
                accum_out=mu2[:, p:p + 1]), waits=xw, inc="ac")
            m1 = E("vector", lambda e, p=par: e.tensor_tensor(
                out=mu[:, p:p + 1], in0=sums2[:, p, 0:1], in1=sums2[:, p, 1:2],
                op=ALU.add), waits=[r1, r2], inc="ve")
            m2 = E("vector", lambda e, p=par: e.tensor_scalar_mul(
                mu[:, p:p + 1], mu[:, p:p + 1], 1.0 / D), waits=[m1], inc="ve")
            v1 = E("vector", lambda e, p=par: e.tensor_scalar(
                out=var_[:, p:p + 1], in0=mu2[:, p:p + 1], scalar1=1.0 / D,
                scalar2=1e-5, op0=ALU.mult, op1=ALU.add), waits=[sq], inc="ve")
            v2 = E("vector", lambda e, p=par: e.tensor_mul(
                mu2[:, p:p + 1], mu[:, p:p + 1], mu[:, p:p + 1]),
                waits=[m2, v1], inc="ve")
            v3 = E("vector", lambda e, p=par: e.tensor_sub(
                var_[:, p:p + 1], var_[:, p:p + 1], mu2[:, p:p + 1]),
                waits=[v2], inc="ve")
            l1 = E("scalar", lambda e, p=par: e.activation(
                out=lnv[:, p:p + 1], in_=var_[:, p:p + 1], func=AF.Ln),
                waits=[v3], inc="ac")
            l2 = E("scalar", lambda e, p=par: e.activation(
                out=rstd[:, p:p + 1], in_=lnv[:, p:p + 1], func=AF.Exp, scale=-0.5),
                waits=[l1], inc="ac")
            nm = E("vector", lambda e, p=par: e.tensor_scalar(
                out=negmur[:, p:p + 1], in0=mu[:, p:p + 1], scalar1=rstd[:, p:p + 1],
                scalar2=-1.0, op0=ALU.mult, op1=ALU.mult), waits=[l2], inc="ve")
            no = E("scalar", lambda e, t=t, p=par: e.activation(
                out=xn_sb[:, p, :], in_=x_res[:, t, :], func=AF.Identity,
                scale=rstd[:, p:p + 1], bias=negmur[:, p:p + 1]),
                waits=[nm, ctx.get(("xnw", par)), ctx.get("brow_done")], inc="ac")
            w = DMA("scalar", xn_s[128 * t:128 * (t + 1), :], xn_sb[:, par, :],
                    waits=[no, ctx.get("xn_strips_done")])
            ctx[("xnw", par)] = w
            wr[t] = w
            if weave:
                weave(t)
        return wr

    def hT(ht):
        return hT_a[:, ht, :] if ht < 16 else hT_b[:, ht - 16, :]

    def ffn_phase(s, xn_wr, first=False, load_wdt=True):
        bank_guard = ctx.get("bank_guard", [])

        def emit_wdt():
            if not load_wdt:
                return
            toks = []
            if first:
                for i in range(HT):
                    toks.append(TDMA(wdt[:, i, :], wd_s[:, 128 * i:128 * (i + 1)],
                                     waits=[wd_hold[i // 4].get("tok")]))
            else:
                for b in range(3):
                    toks.append(DMA("scalar", wdt[:, 8 * b:8 * (b + 1), :],
                                    wdTv[:, 8 * b:8 * (b + 1), :],
                                    waits=[ctx.get("wdt_free"), rw_done["wd"]]))
            ctx["wdt_toks"] = toks

        strip_toks = {}
        win_toks = {}
        win_done = {}
        hsl = {}
        hmm = {}
        dn_evac = {}
        x_upd = [None] * NT

        def emit_strips(g):
            for c in range(DC):
                wt = [xn_wr[4 * g + j] for j in range(TPG)]
                wt.append(win_done.get((g - 2, 5)) if g >= 2 else ctx.get("xnt_free"))
                strip_toks[(g, c)] = TDMA(
                    xnt_f[:, g % 2, c, :],
                    xn_s[GT * g:GT * (g + 1), 128 * c:128 * (c + 1)], waits=wt)

        def emit_windows(g, w):
            par = w % 2
            k = 6 * g + w
            if k >= 2:
                prev = win_done[((k - 2) // 6, (k - 2) % 6)]
            else:
                prev = ctx.get("win_free")
            for name, buf, srcv in (("g", wg_win, wgTv), ("u", wu_win, wuTv)):
                wt = [prev, rw_done["gu"]]
                if first:
                    wt.append(("dP", 16 * PREP_TOTAL))
                win_toks[(g, w, name)] = DMA(
                    "scalar", buf[:, par, :, :],
                    srcv[:, :, 512 * w:512 * (w + 1)], waits=wt)

        def emit_gateup(g, ht):
            w, j = ht // 4, ht % 4
            par = w % 2
            bsel = ht % 2
            pg = ps[:, 512 * bsel:512 * (bsel + 1)]
            pu = ps[:, 1024 + 512 * bsel:1536 + 512 * bsel]
            base_waits = ([win_toks[(g, w, "g")], win_toks[(g, w, "u")]]
                          + [strip_toks[(g, c)] for c in range(DC)])
            if (g, ht - 2) in hsl:
                base_waits += [hsl[(g, ht - 2)], hmm[(g, ht - 2)]]
            elif (g - 1, ht + 22) in hsl:
                base_waits += [hsl[(g - 1, ht + 22)], hmm[(g - 1, ht + 22)]]
            else:
                base_waits += bank_guard
            for c in range(DC):
                E("tensor", lambda e, p=par, c=c, j=j: e.ldweights(
                    wg_win[:, p, c, 128 * j:128 * (j + 1)]),
                    waits=base_waits if c == 0 else [])
                E("tensor", lambda e, ps_=pg, p=par, c=c, j=j, g=g: e.matmul(
                    ps_, wg_win[:, p, c, 128 * j:128 * (j + 1)],
                    xnt_f[:, g % 2, c, :], start=(c == 0), stop=(c == DC - 1)),
                    inc="pe" if c == DC - 1 else None)
            for c in range(DC):
                E("tensor", lambda e, p=par, c=c, j=j: e.ldweights(
                    wu_win[:, p, c, 128 * j:128 * (j + 1)]))
                E("tensor", lambda e, ps_=pu, p=par, c=c, j=j, g=g: e.matmul(
                    ps_, wu_win[:, p, c, 128 * j:128 * (j + 1)],
                    xnt_f[:, g % 2, c, :], start=(c == 0), stop=(c == DC - 1)),
                    inc="pe" if c == DC - 1 else None)
            utok = ("pe", em.cnt["pe"])
            if ht % 4 == 3:
                win_done[(g, w)] = utok
                if ht == HT - 1:
                    win_done[(g, 5)] = utok
            slw = [utok, ctx.get(("silu_busy", bsel))]
            sl = E("scalar", lambda e, b=bsel: e.activation(
                out=silu_sb[:, b, :], in_=ps[:, 512 * b:512 * (b + 1)],
                func=AF.Silu), waits=slw, inc="ac")
            mlw = [sl]
            if ctx.get("hT_guard") is not None and g == 0:
                mlw.append(ctx.get("hT_guard"))
            ml = E("vector", lambda e, h=hT(ht), b=bsel: e.tensor_tensor(
                out=h, in0=silu_sb[:, b, :],
                in1=ps[:, 1024 + 512 * b:1536 + 512 * b], op=ALU.mult),
                waits=mlw, inc="ve")
            ctx[("silu_busy", bsel)] = ml
            hsl[(g, ht)] = sl
            hmm[(g, ht)] = ml

        def emit_down(g):
            for j in range(TPG):
                bsel = j % 2
                py = ps[:, 2048 + 1024 * bsel:3072 + 1024 * bsel]
                tile = 4 * g + j
                wt = ([hmm[(g, ht)] for ht in range(HT)] + wdt_toks)
                if (g, j - 2) in dn_evac:
                    wt.append(dn_evac[(g, j - 2)])
                elif (g - 1, j + 2) in dn_evac:
                    wt.append(dn_evac[(g - 1, j + 2)])
                else:
                    wt += bank_guard
                if first and g == 0 and j == 1:
                    wt.append(ctx.get("b7_dve"))
                for ht_ in range(HT):
                    E("tensor", lambda e, h=hT(ht_), j=j: e.ldweights(
                        h[:, 128 * j:128 * (j + 1)]),
                        waits=wt if ht_ == 0 else [])
                    E("tensor", lambda e, ps_=py, h=hT(ht_), ht=ht_, j=j: e.matmul(
                        ps_[:, 0:512], h[:, 128 * j:128 * (j + 1)],
                        wdt[:, ht, 0:512], start=(ht == 0), stop=False))
                    E("tensor", lambda e, ps_=py, h=hT(ht_), ht=ht_, j=j: e.matmul(
                        ps_[:, 512:1024], h[:, 128 * j:128 * (j + 1)],
                        wdt[:, ht, 512:1024], start=(ht == 0), stop=(ht == HT - 1)),
                        inc="pe" if ht_ == HT - 1 else None)
                mtok = ("pe", em.cnt["pe"])
                E("vector", lambda e, ps_=py, t=tile: e.scalar_tensor_tensor(
                    out=x_res[:, t, 0:512], in0=ps_[:, 0:512], scalar=0.5,
                    in1=x_res[:, t, 0:512], op0=ALU.mult, op1=ALU.add),
                    waits=[mtok], inc="ve")
                e2 = E("vector", lambda e, ps_=py, t=tile: e.scalar_tensor_tensor(
                    out=x_res[:, t, 512:1024], in0=ps_[:, 512:1024], scalar=0.5,
                    in1=x_res[:, t, 512:1024], op0=ALU.mult, op1=ALU.add),
                    inc="ve")
                dn_evac[(g, j)] = e2
                x_upd[tile] = e2

        for g in range(NG):
            emit_strips(g)
            emit_windows(g, 0)
            emit_windows(g, 1)
            if g == 0:
                emit_wdt()
            wdt_toks = ctx["wdt_toks"]
            for ht_ in range(HT):
                emit_gateup(g, ht_)
                if ht_ % 4 == 3 and ht_ // 4 + 2 <= 5:
                    emit_windows(g, ht_ // 4 + 2)
            emit_down(g)

        ctx["xnt_free"] = win_done[(NG - 1, 5)]
        ctx["win_free"] = win_done[(NG - 1, 5)]
        ctx["xn_strips_done"] = strip_toks[(NG - 1, DC - 1)]
        guard = [hsl[(NG - 1, 22)], hsl[(NG - 1, 23)],
                 hmm[(NG - 1, 22)], hmm[(NG - 1, 23)],
                 dn_evac[(NG - 1, 2)], dn_evac[(NG - 1, 3)]]
        ctx["bank_guard"] = guard
        ctx["hT_guard"] = None
        return x_upd, guard

    def qkv_phase(s, xn_wr):
        bank_guard = ctx.get("bank_guard", [])
        xstrips = []
        for c in range(DC):
            xstrips.append(TDMA(xnt[:, c, :], xn_s[:, 128 * c:128 * (c + 1)],
                                waits=xn_wr + [ctx.get("xnt_free")]))
        ctx["xn_strips_done"] = xstrips[-1]
        bias_ld = DMA("scalar", bias_qkv[:, :], b_s[:, :],
                      waits=[bias_tok[0], ctx.get("hTb_done")])
        bqr = bias_qkv[0:1, 0:D]
        bkr = bias_qkv[0:1, D:2 * D]
        bvr = bias_qkv[0:1, 2 * D:3 * D]
        wq_toks, wk_toks = [], []
        if s == 0:
            for c in range(DC):
                wq_toks.append(TDMA(wqt[:, c, :], wqp_s[:, 128 * c:128 * (c + 1)],
                                    waits=[qkv_hold["q"].get("tok"),
                                           ctx.get("win_free")]))
                wk_toks.append(TDMA(wkt[:, c, :], wkp_s[:, 128 * c:128 * (c + 1)],
                                    waits=[qkv_hold["k"].get("tok"),
                                           ctx.get("win_free")]))
        else:
            wq_toks.append(DMA("scalar", wqt[:, :, :], wqTv[:, :, :],
                               waits=[rw_done["wq"], ctx.get("win_free")]))
            wk_toks.append(DMA("scalar", wkt[:, :, :], wkTv[:, :, :],
                               waits=[rw_done["wk"], ctx.get("win_free")]))
        tslice = {}

        def tbl_slice(j):
            par = j % 2
            wt = [tbl_done[0], ctx.get(("tbl_used", par, s)), ctx.get("hTb_done")]
            c0 = DMA("scalar", tbl[:, 0, par, :], tbl_s[0, 128 * j:128 * (j + 1), :],
                     waits=wt)
            c1 = DMA("scalar", tbl[:, 1, par, :], tbl_s[1, 128 * j:128 * (j + 1), :],
                     waits=wt)
            tslice[j] = (c0, c1)

        qk_evac = {}
        qwr, kwr = [], []
        last_qk_pe = None
        tbl_slice(0)
        tbl_slice(1)
        if DBG_STAGE == "qkv" and s == 0:
            DMA("gpsimd", wqt_dbg[:, :, :], wqt[:, :, :], waits=wq_toks)
            DMA("gpsimd", xnt_dbg[:, :, :], xnt[:, :, :], waits=xstrips)
        for j in range(NT):
            par2 = j % 2
            pq = ps[:, 2048 * par2:2048 * par2 + 1024]
            pk = ps[:, 1024 + 2048 * par2:2048 + 2048 * par2]
            mmw = (xstrips + wq_toks + wk_toks
                   + [bias_ld, ctx.get("ones_ready")])
            if j >= 2:
                mmw.append(qk_evac[j - 2])
            else:
                mmw += bank_guard
            E("tensor", lambda e, o=pq: e.matmul(
                o[:, 0:512], ones_bf[:, :], bqr[0:1, 0:512],
                start=True, stop=False), waits=mmw)
            E("tensor", lambda e, o=pq: e.matmul(
                o[:, 512:1024], ones_bf[:, :], bqr[0:1, 512:1024],
                start=True, stop=False))
            E("tensor", lambda e, o=pk: e.matmul(
                o[:, 0:512], ones_bf[:, :], bkr[0:1, 0:512],
                start=True, stop=False))
            E("tensor", lambda e, o=pk: e.matmul(
                o[:, 512:1024], ones_bf[:, :], bkr[0:1, 512:1024],
                start=True, stop=False))
            for c in range(DC):
                E("tensor", lambda e, c=c, j=j: e.ldweights(
                    xnt[:, c, 128 * j:128 * (j + 1)]))
                E("tensor", lambda e, o=pq, c=c, j=j: e.matmul(
                    o[:, 0:512], xnt[:, c, 128 * j:128 * (j + 1)], wqt[:, c, 0:512],
                    start=False, stop=False))
                E("tensor", lambda e, o=pq, c=c, j=j: e.matmul(
                    o[:, 512:1024], xnt[:, c, 128 * j:128 * (j + 1)],
                    wqt[:, c, 512:1024], start=False, stop=(c == DC - 1)))
                E("tensor", lambda e, o=pk, c=c, j=j: e.matmul(
                    o[:, 0:512], xnt[:, c, 128 * j:128 * (j + 1)], wkt[:, c, 0:512],
                    start=False, stop=False))
                E("tensor", lambda e, o=pk, c=c, j=j: e.matmul(
                    o[:, 512:1024], xnt[:, c, 128 * j:128 * (j + 1)],
                    wkt[:, c, 512:1024], start=False, stop=(c == DC - 1)),
                    inc="pe" if c == DC - 1 else None)
            mtok = ("pe", em.cnt["pe"])
            last_qk_pe = mtok
            c_sl = tbl[:, 0, par2, :]
            s_sl = tbl[:, 1, par2, :]
            last_ev = None
            for dst, src, dram, sc in ((qrot, pq, q_s, True), (krot, pk, k_s, False)):
                dwait = ([mtok, tslice[j][0], tslice[j][1],
                          ctx.get(("rot_dma", id(dst)))]
                         + ctx.get("prep_stores", []))
                if DBG_NOROPE:
                    E("vector", lambda e, sp=src, d=dst: e.tensor_copy(
                        out=d[:, 0:512], in_=sp[:, 0:512]), waits=dwait, inc="ve")
                    ev = E("vector", lambda e, sp=src, d=dst: e.tensor_copy(
                        out=d[:, 512:1024], in_=sp[:, 512:1024]), inc="ve")
                    dm = DMA("scalar", dram[128 * j:128 * (j + 1), :], dst[:, :],
                             waits=[ev])
                    ctx[("rot_dma", id(dst))] = dm
                    (qwr if sc else kwr).append(dm)
                    last_ev = ev
                    continue
                E("vector", lambda e, sp=src, c=c_sl: e.tensor_tensor(
                    out=rtmp1[:, :], in0=sp[:, 0:512], in1=c, op=ALU.mult),
                    waits=dwait, inc="ve")
                E("vector", lambda e, sp=src, sn=s_sl: e.tensor_tensor(
                    out=rtmp2[:, :], in0=sp[:, 512:1024], in1=sn, op=ALU.mult),
                    inc="ve")
                E("vector", lambda e, d=dst: e.tensor_sub(
                    d[:, 0:512], rtmp1[:, :], rtmp2[:, :]), inc="ve")
                E("vector", lambda e, sp=src, sn=s_sl: e.tensor_tensor(
                    out=rtmp1[:, :], in0=sp[:, 0:512], in1=sn, op=ALU.mult),
                    inc="ve")
                E("vector", lambda e, sp=src, c=c_sl: e.tensor_tensor(
                    out=rtmp2[:, :], in0=sp[:, 512:1024], in1=c, op=ALU.mult),
                    inc="ve")
                ev = E("vector", lambda e, d=dst: e.tensor_add(
                    d[:, 512:1024], rtmp1[:, :], rtmp2[:, :]), inc="ve")
                dm = DMA("scalar", dram[128 * j:128 * (j + 1), :], dst[:, :],
                         waits=[ev])
                ctx[("rot_dma", id(dst))] = dm
                (qwr if sc else kwr).append(dm)
                last_ev = ev
            qk_evac[j] = last_ev
            ctx[("tbl_used", par2, s)] = last_ev
            if j + 2 < NT:
                tbl_slice(j + 2)
        wv_toks = []
        if s == 0:
            for c in range(DC):
                wv_toks.append(TDMA(wvt[:, c, :], wv_s[:, 128 * c:128 * (c + 1)],
                                    waits=[qkv_hold["v"].get("tok"), last_qk_pe]))
        else:
            wv_toks.append(DMA("scalar", wvt[:, :, :], wvTv[:, :, :],
                               waits=[rw_done["wv"], last_qk_pe]))
        v_evac = {}
        for j in range(NT):
            par2 = j % 2
            pv = ps[:, 2048 * par2:2048 * par2 + 1024]
            mmw = wv_toks + [qk_evac[NT - 2 + (j % 2)] if j < 2 else v_evac[j - 2]]
            E("tensor", lambda e, o=pv: e.matmul(
                o[:, 0:512], ones_bf[:, :], bvr[0:1, 0:512],
                start=True, stop=False), waits=mmw)
            E("tensor", lambda e, o=pv: e.matmul(
                o[:, 512:1024], ones_bf[:, :], bvr[0:1, 512:1024],
                start=True, stop=False))
            for c in range(DC):
                E("tensor", lambda e, c=c, j=j: e.ldweights(
                    xnt[:, c, 128 * j:128 * (j + 1)]))
                E("tensor", lambda e, o=pv, c=c, j=j: e.matmul(
                    o[:, 0:512], xnt[:, c, 128 * j:128 * (j + 1)], wvt[:, c, 0:512],
                    start=False, stop=False))
                E("tensor", lambda e, o=pv, c=c, j=j: e.matmul(
                    o[:, 512:1024], xnt[:, c, 128 * j:128 * (j + 1)],
                    wvt[:, c, 512:1024], start=False, stop=(c == DC - 1)),
                    inc="pe" if c == DC - 1 else None)
            mtok = ("pe", em.cnt["pe"])
            c1 = E("scalar", lambda e, o=pv, j=j: e.activation(
                out=v_res[:, j, 0:512], in_=o[:, 0:512], func=AF.Copy),
                waits=[mtok], inc="ac")
            c2 = E("scalar", lambda e, o=pv, j=j: e.activation(
                out=v_res[:, j, 512:1024], in_=o[:, 512:1024], func=AF.Copy),
                inc="ac")
            v_evac[j] = c2
        last_v_pe = mtok
        ctx["bank_guard"] = [v_evac[NT - 2], v_evac[NT - 1],
                             qk_evac[NT - 2], qk_evac[NT - 1]]
        return qwr, kwr, [v_evac[j] for j in range(NT)], last_v_pe

    def attn_phase(s, qwr, kwr, vev, last_v_pe):
        bank_guard = ctx.get("bank_guard", [])
        kT_toks = []
        for c in range(DC):
            kT_toks.append(TDMA(xnt[:, c, :], k_s[:, 128 * c:128 * (c + 1)],
                                waits=kwr + [ctx.get("xnt_free"), last_v_pe]))
        x_upd = [None] * NT
        exp_toks = {}
        stt_toks = {}
        rc_toks = {}
        qT_toks = {}
        pT_toks = {}
        pdma = {}
        sco_done = {}
        last_pe = [None]

        def emit_qT(g):
            buf = qtb if g % 2 == 0 else qtb_b
            toks = []
            for c in range(DC):
                toks.append(TDMA(buf[:, c, :],
                                 q_s[GT * g:GT * (g + 1), 128 * c:128 * (c + 1)],
                                 waits=[qwr[4 * g + l] for l in range(QPG)]
                                 + [sco_done.get(g - 2), last_v_pe, kwr[-1],
                                    ctx.get("qtb_guard")]))
            qT_toks[g] = toks

        def emit_scores(g, l):
            i = QPG * g + l
            par = i % 2
            qbuf = qtb if g % 2 == 0 else qtb_b
            mmw = qT_toks[g] + kT_toks
            if i >= 1:
                mmw.append(exp_toks[i - 1])
            else:
                mmw += bank_guard
            for c in range(DC):
                E("tensor", lambda e, c=c, l=l, qb=qbuf: e.ldweights(
                    qb[:, c, 128 * l:128 * (l + 1)]),
                    waits=mmw if c == 0 else [])
                for b in range(4):
                    E("tensor", lambda e, c=c, l=l, b=b, qb=qbuf: e.matmul(
                        ps[:, 512 * b:512 * (b + 1)],
                        qb[:, c, 128 * l:128 * (l + 1)],
                        xnt[:, c, 512 * b:512 * (b + 1)],
                        start=(c == 0), stop=(c == DC - 1)),
                        inc="pe" if (c == DC - 1 and b == 3) else None)
            mtok = ("pe", em.cnt["pe"])
            if l == QPG - 1:
                sco_done[g] = mtok
            rm = []
            for b in range(4):
                rm.append(E("vector", lambda e, b=b, p=par: e.reduce_max(
                    out=mx4[:, p, b:b + 1], in_=ps[:, 512 * b:512 * (b + 1)],
                    axis=AXX), waits=[mtok], inc="ve"))
            n1 = E("vector", lambda e, p=par, i=i: e.reduce_max(
                out=nm32[:, i:i + 1], in_=mx4[:, p, :], axis=AXX, negate=True),
                waits=rm, inc="ve")
            n2 = E("vector", lambda e, i=i: e.tensor_scalar_mul(
                nm32[:, i:i + 1], nm32[:, i:i + 1], 1.0 / 32.0), waits=[n1], inc="ve")
            exps = []
            for b in range(4):
                exps.append(E("scalar", lambda e, b=b, i=i, p=par: e.activation(
                    out=p_sb[:, 512 * b:512 * (b + 1)],
                    in_=ps[:, 512 * b:512 * (b + 1)], func=AF.Exp, scale=1.0 / 32.0,
                    bias=nm32[:, i:i + 1], accum_out=rs4[:, p, b:b + 1]),
                    waits=[n2, pdma.get(i - 1)], inc="ac"))
            exp_toks[i] = exps[-1]
            rsum = E("vector", lambda e, p=par, i=i: e.reduce_sum(
                out=rowsum[:, i:i + 1], in_=rs4[:, p, :], axis=AXX),
                waits=exps, inc="ve")
            rc_toks[i] = E("vector", lambda e, i=i: e.reciprocal(
                out=recip[:, i:i + 1], in_=rowsum[:, i:i + 1]), waits=[rsum], inc="ve")
            pdma[i] = DMA("scalar", p_s[128 * l:128 * (l + 1), :], p_sb[:, :],
                          waits=exps + [ctx.get("p_s_free")])

        def emit_pT(g):
            toks = []
            for kc in range(16):
                toks.append(TDMA(pT[:, g % 2, kc, :],
                                 p_s[:, 128 * kc:128 * (kc + 1)],
                                 waits=[pdma[QPG * g + l] for l in range(QPG)]
                                 + [stt_toks.get(QPG * (g - 2) + 3)]))
            pT_toks[g] = toks
            ctx["p_s_free"] = toks[-1]

        def emit_pv(g, l):
            i = QPG * g + l
            par = i % 2
            po = ps[:, 2048 + 1024 * par:3072 + 1024 * par]
            mmw = pT_toks[g] + [vev[-1], rc_toks[i]]
            if i >= 2:
                mmw.append(stt_toks[i - 2])
            else:
                mmw += bank_guard
            for kc in range(16):
                E("tensor", lambda e, kc=kc, l=l, g=g: e.ldweights(
                    pT[:, g % 2, kc, 128 * l:128 * (l + 1)]),
                    waits=mmw if kc == 0 else [])
                E("tensor", lambda e, o=po, kc=kc, g=g, l=l: e.matmul(
                    o[:, 0:512], pT[:, g % 2, kc, 128 * l:128 * (l + 1)],
                    v_res[:, kc, 0:512], start=(kc == 0), stop=False))
                E("tensor", lambda e, o=po, kc=kc, g=g, l=l: e.matmul(
                    o[:, 512:1024], pT[:, g % 2, kc, 128 * l:128 * (l + 1)],
                    v_res[:, kc, 512:1024], start=(kc == 0), stop=(kc == 15)),
                    inc="pe" if kc == 15 else None)
            mtok = ("pe", em.cnt["pe"])
            last_pe[0] = mtok
            E("vector", lambda e, o=po, i=i: e.scalar_tensor_tensor(
                out=x_res[:, i, 0:512], in0=o[:, 0:512], scalar=recip[:, i:i + 1],
                in1=x_res[:, i, 0:512], op0=ALU.mult, op1=ALU.add),
                waits=[mtok, rc_toks[i]], inc="ve")
            s2 = E("vector", lambda e, o=po, i=i: e.scalar_tensor_tensor(
                out=x_res[:, i, 512:1024], in0=o[:, 512:1024],
                scalar=recip[:, i:i + 1], in1=x_res[:, i, 512:1024],
                op0=ALU.mult, op1=ALU.add), inc="ve")
            stt_toks[i] = s2
            x_upd[i] = s2

        emit_qT(0)
        emit_qT(1)
        for g in range(NQG):
            for l in range(QPG):
                emit_scores(g, l)
                if g >= 2:
                    emit_pv(g - 2, l)
            emit_pT(g)
            if g + 2 < NQG:
                emit_qT(g + 2)
        for l in range(QPG):
            emit_pv(NQG - 2, l)
        for l in range(QPG):
            emit_pv(NQG - 1, l)
        ctx["bank_guard"] = [exp_toks[NT - 1], stt_toks[NT - 2], stt_toks[NT - 1]]
        ctx["hT_guard"] = last_pe[0]
        ctx["xnt_free"] = last_pe[0]
        ctx["win_free"] = last_pe[0]
        return x_upd, last_pe[0]

    ctx["ones_ready"] = E("gpsimd", lambda e: e.memset(ones_bf[:, :], 1.0), inc="gp")
    xl0 = x_load(0, "sync")
    emit_gu_prep()
    emit_wd_prep()
    emit_qk_prep("q", wq, wqp_s)
    emit_qk_prep("k", wk, wkp_s)
    emit_v_prep()
    assert prep_state["ci"] == PREP_TOTAL, prep_state["ci"]

    out_final = []
    for s in range(NSEQ):
        if s == 0:
            xl = xl0
            weave = (lambda t: drain_casts(12))
        else:
            xl = x_load(s, "gpsimd")
            weave = None
        xready = [[xl[t // 4]] for t in range(NT)]
        xn1 = ln_phase(s, xready, weave=weave)
        if s == 0:
            drain_casts()
            emit_tables(bank7_guard=[], xn_guard=xn1[NT - 1])
            emit_rewrites_gu()
        up1, guard1 = ffn_phase(s, xn1, first=(s == 0), load_wdt=(s == 0))
        if s == 0:
            ctx["ln1_done"] = xn1[NT - 1]
            emit_bias_prep()
        ctx["hTb_done"] = up1[NT - 1]

        def emit_out(upd):
            xo = x_out[s].rearrange("(t p) d -> p t d", p=128)
            for i in range(4):
                out_final.append(DMA("gpsimd", xo[:, 4 * i:4 * i + 4, :],
                                     x_res[:, 4 * i:4 * (i + 1), :],
                                     waits=[upd[4 * i + j] for j in range(4)]))

        if DBG_STAGE == "ffn1":
            emit_out(up1)
            if s == NSEQ - 1:
                out_final.append(DMA("gpsimd", h_dbg[:, 0:16, :], hT_a[:, :, :],
                                     waits=[up1[NT - 1]]))
                out_final.append(DMA("gpsimd", h_dbg[:, 16:24, :], hT_b[:, :, :],
                                     waits=[up1[NT - 1]]))
            continue
        xn2 = ln_phase(s, [[up1[t]] for t in range(NT)])
        qwr, kwr, vev, lvpe = qkv_phase(s, xn2)
        if s == 0:
            emit_rewrites_wd_qkv(up1[NT - 1])
        if DBG_STAGE == "qkv":
            if s == 0:
                for j in range(NT):
                    out_final.append(DMA("gpsimd",
                                         v_dbg[128 * j:128 * (j + 1), :],
                                         v_res[:, j, :], waits=[vev[j]]))
                out_final.append(qwr[-1])
                out_final.append(kwr[-1])
            emit_out([vev[-1]] * NT)
            ctx["xnt_free"] = lvpe
            ctx["win_free"] = lvpe
            ctx["hT_guard"] = lvpe
            break
        upA, peA = attn_phase(s, qwr, kwr, vev, lvpe)
        if DBG_STAGE == "attn":
            emit_out(upA)
            continue
        xn3 = ln_phase(s, [[upA[t]] for t in range(NT)])
        ctx["wdt_free"] = peA
        up2, _ = ffn_phase(s, xn3, first=False, load_wdt=True)
        ctx["wdt_free"] = None
        emit_out(up2)

    E("gpsimd", lambda e: e.memset(ones_f32[:, :], 1.0), waits=[out_final[-1]])

    with (
        nc.semaphore("s_pe") as s_pe,
        nc.semaphore("s_ve") as s_ve,
        nc.semaphore("s_ac") as s_ac,
        nc.semaphore("s_gp") as s_gp,
        nc.semaphore("s_dW") as s_dW,
        nc.semaphore("s_dS") as s_dS,
        nc.semaphore("s_dG") as s_dG,
        nc.semaphore("s_dP") as s_dP,
        nc.Block() as block,
    ):
        sems = dict(pe=s_pe, ve=s_ve, ac=s_ac, gp=s_gp, dW=s_dW, dS=s_dS,
                    dG=s_dG, dP=s_dP)

        def runner(stream):
            def run(e):
                for waits, build, tok in stream:
                    for sname, val in waits:
                        e.wait_ge(sems[sname], val)
                    ins = build(e)
                    if tok is not None:
                        ins.then_inc(sems[tok[0]],
                                     16 if tok[0] in ("dW", "dS", "dG", "dP") else 1)
            return run

        block.sync(runner(em.streams["sync"]))
        block.scalar(runner(em.streams["scalar"]))
        block.vector(runner(em.streams["vector"]))
        block.tensor(runner(em.streams["tensor"]))
        block.gpsimd(runner(em.streams["gpsimd"]))
    build_program.last_em = em
    return nc


_NC_CACHE = [None]


def kernel(L0_attn_x, L0_attn_wq_w, L0_attn_wq_b, L0_attn_wk_w, L0_attn_wk_b,
           L0_attn_wv_w, L0_attn_wv_b, L0_ffn_w_gate, L0_ffn_w_up, L0_ffn_w_down):
    x = np.ascontiguousarray(np.asarray(L0_attn_x, dtype=np.float32))
    wq_ = np.ascontiguousarray(np.asarray(L0_attn_wq_w, dtype=np.float32))
    wk_ = np.ascontiguousarray(np.asarray(L0_attn_wk_w, dtype=np.float32))
    wv_ = np.ascontiguousarray(np.asarray(L0_attn_wv_w, dtype=np.float32))
    bq_ = np.ascontiguousarray(np.asarray(L0_attn_wq_b, dtype=np.float32)).reshape(1, D)
    bk_ = np.ascontiguousarray(np.asarray(L0_attn_wk_b, dtype=np.float32)).reshape(1, D)
    bv_ = np.ascontiguousarray(np.asarray(L0_attn_wv_b, dtype=np.float32)).reshape(1, D)
    wg_ = np.ascontiguousarray(np.asarray(L0_ffn_w_gate, dtype=np.int32))
    wu_ = np.ascontiguousarray(np.asarray(L0_ffn_w_up, dtype=np.int32))
    wd_ = np.ascontiguousarray(np.asarray(L0_ffn_w_down, dtype=np.int32))

    if _NC_CACHE[0] is None:
        _NC_CACHE[0] = build_program()
    nc = _NC_CACHE[0]

    in_maps = []
    for core in range(8):
        in_maps.append({
            "x_in": x[2 * core:2 * core + 2],
            "wq": wq_, "wk": wk_, "wv": wv_,
            "bq": bq_, "bk": bk_, "bv": bv_,
            "wg": wg_, "wu": wu_, "wd": wd_,
        })
    res = run_bass_kernel_spmd(nc, in_maps, core_ids=list(range(8)))
    out = np.concatenate([r["x_out"] for r in res.results], axis=0)
    return np.ascontiguousarray(out.astype(np.float32))


# version 11 (final)
# speedup vs baseline: 1.0043x; 1.0043x over previous
import numpy as np
import concourse.bass as bass
import concourse.mybir as mybir
from concourse.bass_utils import run_bass_kernel_spmd

F32 = mybir.dt.float32
BF16 = mybir.dt.bfloat16
I32 = mybir.dt.int32
AF = mybir.ActivationFunctionType
ALU = mybir.AluOpType
AXX = mybir.AxisListType.X

S, D, H = 2048, 1024, 3072
NSEQ = 2
NT = S // 128
DC = D // 128
HT = H // 128
NG = 4
GT = S // NG
TPG = GT // 128
NQG = 4
QPG = NT // NQG

PI = float(np.float32(np.pi))
TWO_PI = float(np.float32(2 * np.pi))
INV_2PI = float(np.float32(1.0 / (2 * np.pi)))
CLAMP = PI * 0.9999995
LN1E4 = float(np.log(10000.0))

ENGS = ("sync", "scalar", "vector", "tensor", "gpsimd")

DBG_STAGE = "full"
DBG_NOROPE = False


class Em:
    def __init__(self):
        self.streams = {e: [] for e in ENGS}
        self.cnt = dict(pe=0, ve=0, ac=0, gp=0, dW=0, dS=0, dG=0, dP=0)
        self.watermark = {e: {s: 0 for s in self.cnt} for e in ENGS}

    def op(self, eng, build, waits=(), inc=None, n=1):
        w = {}
        for t in waits:
            if t is None:
                continue
            s, v = t
            if v > self.watermark[eng][s]:
                w[s] = max(w.get(s, 0), v)
        for s, v in w.items():
            self.watermark[eng][s] = v
        tok = None
        if inc is not None:
            self.cnt[inc] += n
            tok = (inc, self.cnt[inc])
        self.streams[eng].append((tuple(w.items()), build, tok))
        return tok

    def dma(self, eng, out_ap, in_ap, waits=(), transpose=False, sem=None):
        if sem is None:
            sem = {"sync": "dW", "scalar": "dS", "gpsimd": "dG"}[eng]
        if transpose:
            b = lambda e, o=out_ap, i=in_ap: e.dma_start(out=o, in_=i, transpose=True)
        else:
            b = lambda e, o=out_ap, i=in_ap: e.dma_start(out=o, in_=i)
        return self.op(eng, b, waits=waits, inc=sem, n=16)


def build_program():
    nc = bass.Bass("TRN2", target_bir_lowering=False)

    x_in = nc.dram_tensor("x_in", [NSEQ, S, D], F32, kind="ExternalInput").ap()
    wq = nc.dram_tensor("wq", [D, D], F32, kind="ExternalInput").ap()
    wk = nc.dram_tensor("wk", [D, D], F32, kind="ExternalInput").ap()
    wv = nc.dram_tensor("wv", [D, D], F32, kind="ExternalInput").ap()
    bq = nc.dram_tensor("bq", [1, D], F32, kind="ExternalInput").ap()
    bk = nc.dram_tensor("bk", [1, D], F32, kind="ExternalInput").ap()
    bv = nc.dram_tensor("bv", [1, D], F32, kind="ExternalInput").ap()
    wg = nc.dram_tensor("wg", [H, D], I32, kind="ExternalInput").ap()
    wu = nc.dram_tensor("wu", [H, D], I32, kind="ExternalInput").ap()
    wd = nc.dram_tensor("wd", [D, H], I32, kind="ExternalInput").ap()
    x_out = nc.dram_tensor("x_out", [NSEQ, S, D], F32, kind="ExternalOutput").ap()

    dbg1 = "ExternalOutput" if DBG_STAGE == "ffn1" else "Internal"
    wg_s = nc.dram_tensor("wg_s", [H, D], BF16, kind=dbg1).ap()
    wu_s = nc.dram_tensor("wu_s", [H, D], BF16, kind=dbg1).ap()
    wd_s = nc.dram_tensor("wd_s", [D, H], BF16, kind=dbg1).ap()
    dbg2 = "ExternalOutput" if DBG_STAGE == "qkv" else "Internal"
    wqp_s = nc.dram_tensor("wqp_s", [D, D], BF16, kind=dbg2).ap()
    wkp_s = nc.dram_tensor("wkp_s", [D, D], BF16, kind=dbg2).ap()
    wv_s = nc.dram_tensor("wv_s", [D, D], BF16).ap()
    tbl_s = nc.dram_tensor("tbl_s", [2, S, 512], F32, kind=dbg2).ap()
    b_s = nc.dram_tensor("b_s", [1, 3 * D], BF16).ap()
    wgT_s = nc.dram_tensor("wgT_s", [D, H], BF16).ap()
    wuT_s = nc.dram_tensor("wuT_s", [D, H], BF16).ap()
    wdT_s = nc.dram_tensor("wdT_s", [H, D], BF16).ap()
    wqT_s = nc.dram_tensor("wqT_s", [D, D], BF16).ap()
    wkT_s = nc.dram_tensor("wkT_s", [D, D], BF16).ap()
    wvT_s = nc.dram_tensor("wvT_s", [D, D], BF16).ap()
    dbg = DBG_STAGE == "qkv"
    xn_s = nc.dram_tensor("xn_s", [S, D], BF16,
                          kind="ExternalOutput" if DBG_STAGE == "ffn1" else "Internal").ap()
    h_dbg = (nc.dram_tensor("h_dbg", [128, HT, 512], BF16, kind="ExternalOutput").ap()
             if DBG_STAGE == "ffn1" else None)
    q_s = nc.dram_tensor("q_s", [S, D], BF16,
                         kind="ExternalOutput" if dbg else "Internal").ap()
    k_s = nc.dram_tensor("k_s", [S, D], BF16,
                         kind="ExternalOutput" if dbg else "Internal").ap()
    p_s = nc.dram_tensor("p_s", [GT, S], BF16).ap()
    v_dbg = (nc.dram_tensor("v_dbg", [S, D], BF16, kind="ExternalOutput").ap()
             if dbg else None)
    wqt_dbg = (nc.dram_tensor("wqt_dbg", [128, DC, D], BF16, kind="ExternalOutput").ap()
               if dbg else None)
    xnt_dbg = (nc.dram_tensor("xnt_dbg", [128, DC, S], BF16, kind="ExternalOutput").ap()
               if dbg else None)

    BASE, TOP = 16512, 229344

    def at(name, shape, dtype, offset):
        return nc.alloc_sbuf_tensor_at(name, shape, dtype, offset=offset).ap()

    o = BASE
    x_res = at("x_res", [128, NT, D], F32, o); o += NT * D * 4
    OFF_XNT = o; o += DC * S * 2
    xnt = at("xnt", [128, DC, S], BF16, OFF_XNT)
    xnt_f = at("xnt_f", [128, 2, DC, 512], BF16, OFF_XNT)
    hT_a = at("hT_a", [128, 16, 512], BF16, OFF_XNT + 16384)
    OFF_A = o; o += HT * D * 2
    wdt = at("wdt", [128, HT, D], BF16, OFF_A)
    v_res = at("v_res", [128, NT, D], BF16, OFF_A)
    qtb = at("qtb", [128, DC, 512], BF16, OFF_A + NT * D * 2)
    bias_qkv = at("bias_qkv", [1, 3 * D], BF16, OFF_A + NT * D * 2 + 8192)
    OFF_WIN = o; o += 32768
    wg_win = at("wg_win", [128, 2, DC, 512], BF16, OFF_WIN)
    rw_stg = at("rw_stg", [128, 2, H], BF16, OFF_WIN)
    rw_qtb = at("rw_qtb", [128, 2, D], BF16, OFF_A + NT * D * 2)
    stg8_i = at("stg8_i", [128, 8, 512], I32, OFF_WIN + 12288)
    stg8_f = at("stg8_f", [128, 8, 512], F32, OFF_WIN + 12288)
    stg4_b = at("stg4_b", [128, 4, 512], BF16, OFF_WIN + 28672)
    wu_win = at("wu_win", [128, 2, DC, 512], BF16, OFF_WIN + 16384)
    wqt = at("wqt", [128, DC, D], BF16, OFF_WIN)
    wkt = at("wkt", [128, DC, D], BF16, OFF_WIN + 16384)
    wvt = at("wvt", [128, DC, D], BF16, OFF_WIN)
    pT = at("pT", [128, 2, 16, 512], BF16, OFF_WIN)
    OFF_TBL = o; o += 8192
    tbl = at("tbl", [128, 2, 2, 512], F32, OFF_TBL)
    hT_b = at("hT_b", [128, 8, 512], BF16, OFF_TBL)
    invf_bc = at("invf_bc", [128, 512], F32, OFF_TBL)
    invf = at("invf", [1, 512], F32, OFF_TBL + 2048)
    jrow_f = at("jrow_f", [1, 512], F32, OFF_TBL + 4096)
    OFF_S1 = o; o += 4096
    p_sb = at("p_sb", [128, S], BF16, OFF_S1)
    OFF_S2 = o; o += 4096
    silu_sb = at("silu_sb", [128, 2, 512], BF16, OFF_S2)
    qtb_b = at("qtb_b", [128, DC, 512], BF16, OFF_S2)
    rtmp1 = at("rtmp1", [128, 512], F32, OFF_S2)
    rtmp2 = at("rtmp2", [128, 512], F32, OFF_S2 + 2048)
    OFF_QROT = o; o += 4096
    qrot = at("qrot", [128, D], BF16, OFF_QROT)
    krot = at("krot", [128, D], BF16, OFF_QROT + 2048)
    stage_b = at("stage_b", [128, 2, 512], BF16, OFF_QROT)
    tg_arg = at("tg_arg", [128, 512], F32, OFF_QROT + 2048)
    OFF_STG = o; o += 4096
    stage_i = at("stage_i", [128, 2, 512], I32, OFF_STG)
    stage_f = at("stage_f", [128, 2, 512], F32, OFF_STG)
    tg_ang = at("tg_ang", [128, 512], F32, OFF_STG)
    tg_i32 = at("tg_i32", [128, 512], I32, OFF_STG + 2048)
    brow_f32 = at("brow_f32", [1, D], F32, OFF_STG)
    OFF_SQ = o; o += 4096
    sq_sb = at("sq_sb", [128, D], BF16, OFF_SQ)
    jrow_i = at("jrow_i", [1, 512], I32, OFF_SQ + 2048)
    brow_bf = at("brow_bf", [1, D], BF16, o); o += 2048
    xn_sb = at("xn_sb", [128, 2, D], BF16, OFF_SQ + 2048)
    ones_bf = at("ones_bf", [1, 128], BF16, o); o += 288
    ones_f32 = at("ones_f32", [1, 128], F32, o); o += 512
    pos_i = at("pos_i", [128, NT], I32, o); o += 64
    pos_f = at("pos_f", [128, NT], F32, o); o += 64
    sums2 = at("sums2", [128, 2, 2], F32, o); o += 32
    mu = at("mu", [128, 2], F32, o); o += 32
    mu2 = at("mu2", [128, 2], F32, o); o += 32
    var_ = at("var_", [128, 2], F32, o); o += 32
    lnv = at("lnv", [128, 2], F32, o); o += 32
    rstd = at("rstd", [128, 2], F32, o); o += 32
    negmur = at("negmur", [128, 2], F32, o); o += 32
    mx4 = at("mx4", [128, 2, 4], F32, o); o += 64
    rs4 = at("rs4", [128, 2, 4], F32, o); o += 64
    nm32 = at("nm32", [128, NT], F32, o); o += 64
    rowsum = at("rowsum", [128, NT], F32, o); o += 64
    recip = at("recip", [128, NT], F32, o); o += 64
    assert o <= TOP, f"SBUF overflow: {o} > {TOP}"

    ps = nc.alloc_psum_tensor("ps", [128, 4096], F32).ap()

    em = Em()
    E = em.op
    DMA = em.dma
    ctx = {}
    def TDMA(out_ap, in_ap, waits=()):
        return em.dma("sync", out_ap, in_ap, waits=waits, transpose=True)

    wgTv = wgT_s.rearrange("(c p) h -> p c h", p=128)
    wuTv = wuT_s.rearrange("(c p) h -> p c h", p=128)
    wdTv = wdT_s.rearrange("(i p) d -> p i d", p=128)
    wqTv = wqT_s.rearrange("(c p) e -> p c e", p=128)
    wkTv = wkT_s.rearrange("(c p) e -> p c e", p=128)
    wvTv = wvT_s.rearrange("(c p) e -> p c e", p=128)

    prep_state = {"st": {}, "ci": 0, "q": 0}
    pending_casts = []

    def prep_chunk(load_src, store_dst, scale):
        ci = prep_state["ci"]
        prep_state["ci"] += 1
        par8 = ci % 8
        par4 = ci % 4
        cols = load_src.shape[-1]
        is_int = load_src.dtype == I32
        lw = [("dP", 16 * (ci - 7))] if ci >= 8 else []
        ld = DMA("gpsimd", stg8_i[:, par8, 0:cols] if is_int else stg8_f[:, par8, 0:cols],
                 load_src, waits=lw)
        holder = {}

        def cast():
            srcap = stg8_i[:, par8, 0:cols] if is_int else stg8_f[:, par8, 0:cols]
            cv = E("scalar", lambda e, s=srcap, p=par4, c=cols, sc=scale:
                   e.activation(out=stg4_b[:, p, 0:c], in_=s, func=AF.Copy, scale=sc),
                   waits=[ld, prep_state["st"].get(par4)], inc="ac")
            st = DMA("scalar", store_dst, stg4_b[:, par4, 0:cols], waits=[cv],
                     sem="dP")
            prep_state["st"][par4] = st
            holder["tok"] = st
        pending_casts.append(cast)
        return holder

    def drain_casts(k=10 ** 9):
        n = 0
        while pending_casts and n < k:
            pending_casts.pop(0)()
            n += 1

    wd_hold = []
    def emit_wd_prep():
        for j in range(6):
            h = None
            for i in range(8):
                h = prep_chunk(wd[128 * i:128 * (i + 1), 512 * j:512 * (j + 1)],
                               wd_s[128 * i:128 * (i + 1), 512 * j:512 * (j + 1)],
                               1.0 / 127.0)
            wd_hold.append(h)

    gu_hold = {}
    def emit_gu_prep():
        for name, src, dst in (("g", wg, wg_s), ("u", wu, wu_s)):
            for w in range(6):
                h = None
                for i in range(4):
                    r = 512 * w + 128 * i
                    for j in range(2):
                        h = prep_chunk(src[r:r + 128, 512 * j:512 * (j + 1)],
                                       dst[r:r + 128, 512 * j:512 * (j + 1)],
                                       1.0 / 127.0)
                gu_hold[(name, w)] = h

    qkv_hold = {}
    def emit_qk_prep(name, src, dst):
        h = None
        for half in range(2):
            for i in range(4):
                for j in range(2):
                    h = prep_chunk(
                        src[half + 256 * i:half + 256 * i + 255:2,
                            512 * j:512 * (j + 1)],
                        dst[512 * half + 128 * i:512 * half + 128 * (i + 1),
                            512 * j:512 * (j + 1)],
                        1.0)
        qkv_hold[name] = h

    def emit_v_prep():
        h = None
        for i in range(8):
            for j in range(2):
                h = prep_chunk(wv[128 * i:128 * (i + 1), 512 * j:512 * (j + 1)],
                               wv_s[128 * i:128 * (i + 1), 512 * j:512 * (j + 1)],
                               1.0)
        qkv_hold["v"] = h

    bias_tok = [None]
    def emit_bias_prep():
        drain_casts()
        t = prep_state["st"].get(0)
        t2 = prep_state["st"].get(3)
        ctx["prep_stores"] = [prep_state["st"].get(i) for i in range(4)]
        wprev = tbl_done[0]
        for idx, (src, perm) in enumerate(((bq, True), (bk, True), (bv, False))):
            ld = DMA("gpsimd", brow_f32[:, :], src[:, :],
                     waits=[t, t2, wprev, ctx.get("ln1_done")])
            if perm:
                E("vector", lambda e: e.tensor_copy(
                    out=brow_bf[0:1, 0:512], in_=brow_f32[0:1, 0:D:2]),
                    waits=[ld, wprev], inc="ve")
                t = E("vector", lambda e: e.tensor_copy(
                    out=brow_bf[0:1, 512:1024], in_=brow_f32[0:1, 1:D:2]),
                    waits=[ld], inc="ve")
            else:
                t = E("vector", lambda e: e.tensor_copy(
                    out=brow_bf[0:1, :], in_=brow_f32[0:1, :]),
                    waits=[ld, wprev], inc="ve")
            t2 = t
            wprev = DMA("scalar", b_s[0:1, D * idx:D * (idx + 1)], brow_bf[0:1, :],
                        waits=[t])
        bias_tok[0] = wprev
        ctx["brow_done"] = wprev

    PREP_TOTAL = 192
    rw_done = {}
    def rewrite_one(name, nat, dst, rows, cols, stage, extra, rwtok):
        n = cols // 128
        rds = []
        for c in range(n + 1):
            par = c % 2
            cc = min(c, n - 1)
            rd = TDMA(stage[:, par, 0:rows], nat[:, 128 * cc:128 * (cc + 1)],
                      waits=[rwtok.get(par)] + extra)
            rds.append(rd)
            if c >= 1:
                wr = DMA("scalar", dst[128 * (c - 1):128 * c, :],
                         stage[:, (c - 1) % 2, 0:rows], waits=[rds[c - 1], rd])
                rwtok[(c - 1) % 2] = wr
        rw_done[name] = wr
        return wr

    def emit_rewrites_gu():
        rwtok = {}
        rewrite_one("wg", wg_s, wgT_s, H, D, rw_stg,
                    [gu_hold[("g", 5)].get("tok")], rwtok)
        rewrite_one("wu", wu_s, wuT_s, H, D, rw_stg,
                    [gu_hold[("u", 5)].get("tok")], rwtok)
        rw_done["gu"] = rw_done["wu"]

    def emit_rewrites_wd_qkv(x_done):
        rwtok = {}
        rewrite_one("wd", wd_s, wdT_s, D, H, rw_qtb,
                    [x_done, wd_hold[5].get("tok")], rwtok)
        rewrite_one("wq", wqp_s, wqT_s, D, D, rw_qtb,
                    [qkv_hold["q"].get("tok")], rwtok)
        rewrite_one("wk", wkp_s, wkT_s, D, D, rw_qtb,
                    [qkv_hold["k"].get("tok")], rwtok)
        last = rewrite_one("wv", wv_s, wvT_s, D, D, rw_qtb,
                           [qkv_hold["v"].get("tok")], rwtok)
        ctx["qtb_guard"] = last

    tbl_done = [None]
    def emit_tables(bank7_guard, xn_guard=None):
        iota1 = E("gpsimd", lambda e: e.iota(jrow_i[:, :], pattern=[[1, 512]], base=0,
                                             channel_multiplier=0),
                  waits=[xn_guard], inc="gp")
        iota2 = E("gpsimd", lambda e: e.iota(pos_i[:, :], pattern=[[128, NT]], base=0,
                                             channel_multiplier=1), inc="gp")
        onf = E("gpsimd", lambda e: e.memset(ones_f32[:, :], 1.0), inc="gp")
        jf = E("vector", lambda e: e.tensor_copy(out=jrow_f[:, :], in_=jrow_i[:, :]),
               waits=[iota1] + bank7_guard, inc="ve")
        pf = E("vector", lambda e: e.tensor_copy(out=pos_f[:, :], in_=pos_i[:, :]),
               waits=[iota2], inc="ve")
        inv = E("scalar", lambda e: e.activation(out=invf[:, :], in_=jrow_f[:, :],
                                                 func=AF.Exp, scale=-LN1E4 / 512.0),
                waits=[jf], inc="ac")
        bcm = E("tensor", lambda e: e.matmul(ps[:, 3584:4096], ones_f32[:, :],
                                             invf[:, :], start=True, stop=True),
                waits=[inv, onf] + bank7_guard, inc="pe")
        bcc = E("vector", lambda e: e.tensor_copy(out=invf_bc[:, :],
                                                  in_=ps[:, 3584:4096]),
                waits=[bcm], inc="ve")
        ctx["b7_dve"] = bcc
        prev = {}
        last = None
        for t in range(NT):
            a1 = E("vector", lambda e, t=t: e.tensor_scalar_mul(
                tg_ang[:, :], invf_bc[:, :], pos_f[:, t:t + 1]),
                waits=[bcc, pf, prev.get(1)], inc="ve")
            for which, src, shift in ((0, tg_arg, PI / 2), (1, tg_ang, 0.0)):
                if shift:
                    s1 = E("vector", lambda e: e.tensor_scalar_add(
                        tg_arg[:, :], tg_ang[:, :], PI / 2),
                        waits=[a1, prev.get(0)], inc="ve")
                else:
                    s1 = a1
                k1 = E("vector", lambda e, sr=src: e.tensor_scalar_mul(
                    tg_i32[:, :], sr[:, :], INV_2PI), waits=[s1], inc="ve")
                r1 = E("vector", lambda e, sr=src: e.scalar_tensor_tensor(
                    out=sr[:, :], in0=tg_i32[:, :], scalar=-TWO_PI, in1=sr[:, :],
                    op0=ALU.mult, op1=ALU.add), waits=[k1], inc="ve")
                c1 = E("vector", lambda e, sr=src: e.tensor_scalar(
                    out=sr[:, :], in0=sr[:, :], scalar1=CLAMP, scalar2=-CLAMP,
                    op0=ALU.min, op1=ALU.max), waits=[r1], inc="ve")
                sn = E("scalar", lambda e, sr=src: e.activation(
                    out=sr[:, :], in_=sr[:, :], func=AF.Sin), waits=[c1], inc="ac")
                dm = DMA("scalar", tbl_s[which, 128 * t:128 * (t + 1), :], src[:, :],
                         waits=[sn])
                prev[which] = dm
                last = dm
        tbl_done[0] = last
        return bcc

    def x_load(s, eng):
        toks = []
        xr = x_in[s].rearrange("(t p) d -> p t d", p=128)
        for i in range(4):
            toks.append(DMA(eng, x_res[:, 4 * i:4 * (i + 1), :],
                            xr[:, 4 * i:4 * i + 4, :]))
        return toks

    def ln_phase(s, xready, weave=None):
        wr = [None] * NT
        for t in range(NT):
            par = t % 2
            xw = list(xready[t])
            r1 = E("vector", lambda e, t=t, p=par: e.reduce_sum(
                out=sums2[:, p, 0:1], in_=x_res[:, t, 0:512], axis=AXX),
                waits=xw, inc="ve")
            r2 = E("vector", lambda e, t=t, p=par: e.reduce_sum(
                out=sums2[:, p, 1:2], in_=x_res[:, t, 512:1024], axis=AXX),
                waits=xw, inc="ve")
            sq = E("scalar", lambda e, t=t, p=par: e.activation(
                out=sq_sb[:, :], in_=x_res[:, t, :], func=AF.Square,
                accum_out=mu2[:, p:p + 1]), waits=xw, inc="ac")
            m1 = E("vector", lambda e, p=par: e.tensor_tensor(
                out=mu[:, p:p + 1], in0=sums2[:, p, 0:1], in1=sums2[:, p, 1:2],
                op=ALU.add), waits=[r1, r2], inc="ve")
            m2 = E("vector", lambda e, p=par: e.tensor_scalar_mul(
                mu[:, p:p + 1], mu[:, p:p + 1], 1.0 / D), waits=[m1], inc="ve")
            v1 = E("vector", lambda e, p=par: e.tensor_scalar(
                out=var_[:, p:p + 1], in0=mu2[:, p:p + 1], scalar1=1.0 / D,
                scalar2=1e-5, op0=ALU.mult, op1=ALU.add), waits=[sq], inc="ve")
            v2 = E("vector", lambda e, p=par: e.tensor_mul(
                mu2[:, p:p + 1], mu[:, p:p + 1], mu[:, p:p + 1]),
                waits=[m2, v1], inc="ve")
            v3 = E("vector", lambda e, p=par: e.tensor_sub(
                var_[:, p:p + 1], var_[:, p:p + 1], mu2[:, p:p + 1]),
                waits=[v2], inc="ve")
            l1 = E("scalar", lambda e, p=par: e.activation(
                out=lnv[:, p:p + 1], in_=var_[:, p:p + 1], func=AF.Ln),
                waits=[v3], inc="ac")
            l2 = E("scalar", lambda e, p=par: e.activation(
                out=rstd[:, p:p + 1], in_=lnv[:, p:p + 1], func=AF.Exp, scale=-0.5),
                waits=[l1], inc="ac")
            nm = E("vector", lambda e, p=par: e.tensor_scalar(
                out=negmur[:, p:p + 1], in0=mu[:, p:p + 1], scalar1=rstd[:, p:p + 1],
                scalar2=-1.0, op0=ALU.mult, op1=ALU.mult), waits=[l2], inc="ve")
            no = E("scalar", lambda e, t=t, p=par: e.activation(
                out=xn_sb[:, p, :], in_=x_res[:, t, :], func=AF.Identity,
                scale=rstd[:, p:p + 1], bias=negmur[:, p:p + 1]),
                waits=[nm, ctx.get(("xnw", par)), ctx.get("brow_done")], inc="ac")
            w = DMA("scalar", xn_s[128 * t:128 * (t + 1), :], xn_sb[:, par, :],
                    waits=[no, ctx.get("xn_strips_done")])
            ctx[("xnw", par)] = w
            wr[t] = w
            if weave:
                weave(t)
        return wr

    def hT(ht):
        return hT_a[:, ht, :] if ht < 16 else hT_b[:, ht - 16, :]

    def ffn_phase(s, xn_wr, first=False, load_wdt=True):
        bank_guard = ctx.get("bank_guard", [])

        def emit_wdt():
            if not load_wdt:
                return
            toks = []
            if first:
                for i in range(HT):
                    toks.append(TDMA(wdt[:, i, :], wd_s[:, 128 * i:128 * (i + 1)],
                                     waits=[wd_hold[i // 4].get("tok")]))
            else:
                for b in range(3):
                    toks.append(DMA("scalar", wdt[:, 8 * b:8 * (b + 1), :],
                                    wdTv[:, 8 * b:8 * (b + 1), :],
                                    waits=[ctx.get("wdt_free"), rw_done["wd"]]))
            ctx["wdt_toks"] = toks

        strip_toks = {}
        win_toks = {}
        win_done = {}
        hsl = {}
        hmm = {}
        dn_evac = {}
        x_upd = [None] * NT

        def emit_strips(g):
            for c in range(DC):
                wt = [xn_wr[4 * g + j] for j in range(TPG)]
                wt.append(win_done.get((g - 2, 5)) if g >= 2 else ctx.get("xnt_free"))
                strip_toks[(g, c)] = TDMA(
                    xnt_f[:, g % 2, c, :],
                    xn_s[GT * g:GT * (g + 1), 128 * c:128 * (c + 1)], waits=wt)

        def emit_windows(g, w):
            par = w % 2
            k = 6 * g + w
            if k >= 2:
                prev = win_done[((k - 2) // 6, (k - 2) % 6)]
            else:
                prev = ctx.get("win_free")
            for name, buf, srcv in (("g", wg_win, wgTv), ("u", wu_win, wuTv)):
                wt = [prev, rw_done["gu"]]
                if first:
                    wt.append(("dP", 16 * PREP_TOTAL))
                win_toks[(g, w, name)] = DMA(
                    "scalar", buf[:, par, :, :],
                    srcv[:, :, 512 * w:512 * (w + 1)], waits=wt)

        def emit_gateup(g, ht):
            w, j = ht // 4, ht % 4
            par = w % 2
            bsel = ht % 2
            pg = ps[:, 512 * bsel:512 * (bsel + 1)]
            pu = ps[:, 1024 + 512 * bsel:1536 + 512 * bsel]
            base_waits = ([win_toks[(g, w, "g")], win_toks[(g, w, "u")]]
                          + [strip_toks[(g, c)] for c in range(DC)])
            if (g, ht - 2) in hsl:
                base_waits += [hsl[(g, ht - 2)], hmm[(g, ht - 2)]]
            elif (g - 1, ht + 22) in hsl:
                base_waits += [hsl[(g - 1, ht + 22)], hmm[(g - 1, ht + 22)]]
            else:
                base_waits += bank_guard
            for c in range(DC):
                E("tensor", lambda e, p=par, c=c, j=j: e.ldweights(
                    wg_win[:, p, c, 128 * j:128 * (j + 1)]),
                    waits=base_waits if c == 0 else [])
                E("tensor", lambda e, ps_=pg, p=par, c=c, j=j, g=g: e.matmul(
                    ps_, wg_win[:, p, c, 128 * j:128 * (j + 1)],
                    xnt_f[:, g % 2, c, :], start=(c == 0), stop=(c == DC - 1)),
                    inc="pe" if c == DC - 1 else None)
            for c in range(DC):
                E("tensor", lambda e, p=par, c=c, j=j: e.ldweights(
                    wu_win[:, p, c, 128 * j:128 * (j + 1)]))
                E("tensor", lambda e, ps_=pu, p=par, c=c, j=j, g=g: e.matmul(
                    ps_, wu_win[:, p, c, 128 * j:128 * (j + 1)],
                    xnt_f[:, g % 2, c, :], start=(c == 0), stop=(c == DC - 1)),
                    inc="pe" if c == DC - 1 else None)
            utok = ("pe", em.cnt["pe"])
            if ht % 4 == 3:
                win_done[(g, w)] = utok
                if ht == HT - 1:
                    win_done[(g, 5)] = utok
            slw = [utok, ctx.get(("silu_busy", bsel))]
            sl = E("scalar", lambda e, b=bsel: e.activation(
                out=silu_sb[:, b, :], in_=ps[:, 512 * b:512 * (b + 1)],
                func=AF.Silu), waits=slw, inc="ac")
            mlw = [sl]
            if ctx.get("hT_guard") is not None and g == 0:
                mlw.append(ctx.get("hT_guard"))
            ml = E("vector", lambda e, h=hT(ht), b=bsel: e.tensor_tensor(
                out=h, in0=silu_sb[:, b, :],
                in1=ps[:, 1024 + 512 * b:1536 + 512 * b], op=ALU.mult),
                waits=mlw, inc="ve")
            ctx[("silu_busy", bsel)] = ml
            hsl[(g, ht)] = sl
            hmm[(g, ht)] = ml

        def emit_down(g):
            for j in range(TPG):
                bsel = j % 2
                py = ps[:, 2048 + 1024 * bsel:3072 + 1024 * bsel]
                tile = 4 * g + j
                wt = ([hmm[(g, ht)] for ht in range(HT)] + wdt_toks)
                if (g, j - 2) in dn_evac:
                    wt.append(dn_evac[(g, j - 2)])
                elif (g - 1, j + 2) in dn_evac:
                    wt.append(dn_evac[(g - 1, j + 2)])
                else:
                    wt += bank_guard
                if first and g == 0 and j == 1:
                    wt.append(ctx.get("b7_dve"))
                for ht_ in range(HT):
                    E("tensor", lambda e, h=hT(ht_), j=j: e.ldweights(
                        h[:, 128 * j:128 * (j + 1)]),
                        waits=wt if ht_ == 0 else [])
                    E("tensor", lambda e, ps_=py, h=hT(ht_), ht=ht_, j=j: e.matmul(
                        ps_[:, 0:512], h[:, 128 * j:128 * (j + 1)],
                        wdt[:, ht, 0:512], start=(ht == 0), stop=False))
                    E("tensor", lambda e, ps_=py, h=hT(ht_), ht=ht_, j=j: e.matmul(
                        ps_[:, 512:1024], h[:, 128 * j:128 * (j + 1)],
                        wdt[:, ht, 512:1024], start=(ht == 0), stop=(ht == HT - 1)),
                        inc="pe" if ht_ == HT - 1 else None)
                mtok = ("pe", em.cnt["pe"])
                E("vector", lambda e, ps_=py, t=tile: e.scalar_tensor_tensor(
                    out=x_res[:, t, 0:512], in0=ps_[:, 0:512], scalar=0.5,
                    in1=x_res[:, t, 0:512], op0=ALU.mult, op1=ALU.add),
                    waits=[mtok], inc="ve")
                e2 = E("vector", lambda e, ps_=py, t=tile: e.scalar_tensor_tensor(
                    out=x_res[:, t, 512:1024], in0=ps_[:, 512:1024], scalar=0.5,
                    in1=x_res[:, t, 512:1024], op0=ALU.mult, op1=ALU.add),
                    inc="ve")
                dn_evac[(g, j)] = e2
                x_upd[tile] = e2

        for g in range(NG):
            emit_strips(g)
            emit_windows(g, 0)
            emit_windows(g, 1)
            if g == 0:
                emit_wdt()
            wdt_toks = ctx["wdt_toks"]
            for ht_ in range(HT):
                emit_gateup(g, ht_)
                if ht_ % 4 == 3 and ht_ // 4 + 2 <= 5:
                    emit_windows(g, ht_ // 4 + 2)
            emit_down(g)

        ctx["xnt_free"] = win_done[(NG - 1, 5)]
        ctx["win_free"] = win_done[(NG - 1, 5)]
        ctx["xn_strips_done"] = strip_toks[(NG - 1, DC - 1)]
        guard = [hsl[(NG - 1, 22)], hsl[(NG - 1, 23)],
                 hmm[(NG - 1, 22)], hmm[(NG - 1, 23)],
                 dn_evac[(NG - 1, 2)], dn_evac[(NG - 1, 3)]]
        ctx["bank_guard"] = guard
        ctx["hT_guard"] = None
        return x_upd, guard

    def qkv_phase(s, xn_wr):
        bank_guard = ctx.get("bank_guard", [])
        wq_toks, wk_toks = [], []
        if s == 0:
            for c in range(DC):
                wq_toks.append(TDMA(wqt[:, c, :], wqp_s[:, 128 * c:128 * (c + 1)],
                                    waits=[qkv_hold["q"].get("tok"),
                                           ctx.get("win_free")]))
                wk_toks.append(TDMA(wkt[:, c, :], wkp_s[:, 128 * c:128 * (c + 1)],
                                    waits=[qkv_hold["k"].get("tok"),
                                           ctx.get("win_free")]))
        else:
            wq_toks.append(DMA("scalar", wqt[:, :, :], wqTv[:, :, :],
                               waits=[rw_done["wq"], ctx.get("win_free")]))
            wk_toks.append(DMA("scalar", wkt[:, :, :], wkTv[:, :, :],
                               waits=[rw_done["wk"], ctx.get("win_free")]))
        xstrips = []
        for c in range(DC):
            xstrips.append(TDMA(xnt[:, c, :], xn_s[:, 128 * c:128 * (c + 1)],
                                waits=xn_wr + [ctx.get("xnt_free")]))
        ctx["xn_strips_done"] = xstrips[-1]
        bias_ld = DMA("scalar", bias_qkv[:, :], b_s[:, :],
                      waits=[bias_tok[0], ctx.get("hTb_done")])
        bqr = bias_qkv[0:1, 0:D]
        bkr = bias_qkv[0:1, D:2 * D]
        bvr = bias_qkv[0:1, 2 * D:3 * D]
        tslice = {}

        def tbl_slice(j):
            par = j % 2
            wt = [tbl_done[0], ctx.get(("tbl_used", par, s)), ctx.get("hTb_done")]
            c0 = DMA("scalar", tbl[:, 0, par, :], tbl_s[0, 128 * j:128 * (j + 1), :],
                     waits=wt)
            c1 = DMA("scalar", tbl[:, 1, par, :], tbl_s[1, 128 * j:128 * (j + 1), :],
                     waits=wt)
            tslice[j] = (c0, c1)

        qk_evac = {}
        qwr, kwr = [], []
        last_qk_pe = None
        tbl_slice(0)
        tbl_slice(1)
        if DBG_STAGE == "qkv" and s == 0:
            DMA("gpsimd", wqt_dbg[:, :, :], wqt[:, :, :], waits=wq_toks)
            DMA("gpsimd", xnt_dbg[:, :, :], xnt[:, :, :], waits=xstrips)
        for j in range(NT):
            par2 = j % 2
            pq = ps[:, 2048 * par2:2048 * par2 + 1024]
            pk = ps[:, 1024 + 2048 * par2:2048 + 2048 * par2]
            mmw = (xstrips + wq_toks + wk_toks
                   + [bias_ld, ctx.get("ones_ready")])
            if j >= 2:
                mmw.append(qk_evac[j - 2])
            else:
                mmw += bank_guard
            E("tensor", lambda e, o=pq: e.matmul(
                o[:, 0:512], ones_bf[:, :], bqr[0:1, 0:512],
                start=True, stop=False), waits=mmw)
            E("tensor", lambda e, o=pq: e.matmul(
                o[:, 512:1024], ones_bf[:, :], bqr[0:1, 512:1024],
                start=True, stop=False))
            E("tensor", lambda e, o=pk: e.matmul(
                o[:, 0:512], ones_bf[:, :], bkr[0:1, 0:512],
                start=True, stop=False))
            E("tensor", lambda e, o=pk: e.matmul(
                o[:, 512:1024], ones_bf[:, :], bkr[0:1, 512:1024],
                start=True, stop=False))
            for c in range(DC):
                E("tensor", lambda e, c=c, j=j: e.ldweights(
                    xnt[:, c, 128 * j:128 * (j + 1)]))
                E("tensor", lambda e, o=pq, c=c, j=j: e.matmul(
                    o[:, 0:512], xnt[:, c, 128 * j:128 * (j + 1)], wqt[:, c, 0:512],
                    start=False, stop=False))
                E("tensor", lambda e, o=pq, c=c, j=j: e.matmul(
                    o[:, 512:1024], xnt[:, c, 128 * j:128 * (j + 1)],
                    wqt[:, c, 512:1024], start=False, stop=(c == DC - 1)))
                E("tensor", lambda e, o=pk, c=c, j=j: e.matmul(
                    o[:, 0:512], xnt[:, c, 128 * j:128 * (j + 1)], wkt[:, c, 0:512],
                    start=False, stop=False))
                E("tensor", lambda e, o=pk, c=c, j=j: e.matmul(
                    o[:, 512:1024], xnt[:, c, 128 * j:128 * (j + 1)],
                    wkt[:, c, 512:1024], start=False, stop=(c == DC - 1)),
                    inc="pe" if c == DC - 1 else None)
            mtok = ("pe", em.cnt["pe"])
            last_qk_pe = mtok
            c_sl = tbl[:, 0, par2, :]
            s_sl = tbl[:, 1, par2, :]
            last_ev = None
            for dst, src, dram, sc in ((qrot, pq, q_s, True), (krot, pk, k_s, False)):
                dwait = ([mtok, tslice[j][0], tslice[j][1],
                          ctx.get(("rot_dma", id(dst)))]
                         + ctx.get("prep_stores", []))
                if DBG_NOROPE:
                    E("vector", lambda e, sp=src, d=dst: e.tensor_copy(
                        out=d[:, 0:512], in_=sp[:, 0:512]), waits=dwait, inc="ve")
                    ev = E("vector", lambda e, sp=src, d=dst: e.tensor_copy(
                        out=d[:, 512:1024], in_=sp[:, 512:1024]), inc="ve")
                    dm = DMA("scalar", dram[128 * j:128 * (j + 1), :], dst[:, :],
                             waits=[ev])
                    ctx[("rot_dma", id(dst))] = dm
                    (qwr if sc else kwr).append(dm)
                    last_ev = ev
                    continue
                E("vector", lambda e, sp=src, c=c_sl: e.tensor_tensor(
                    out=rtmp1[:, :], in0=sp[:, 0:512], in1=c, op=ALU.mult),
                    waits=dwait, inc="ve")
                E("vector", lambda e, sp=src, sn=s_sl: e.tensor_tensor(
                    out=rtmp2[:, :], in0=sp[:, 512:1024], in1=sn, op=ALU.mult),
                    inc="ve")
                E("vector", lambda e, d=dst: e.tensor_sub(
                    d[:, 0:512], rtmp1[:, :], rtmp2[:, :]), inc="ve")
                E("vector", lambda e, sp=src, sn=s_sl: e.tensor_tensor(
                    out=rtmp1[:, :], in0=sp[:, 0:512], in1=sn, op=ALU.mult),
                    inc="ve")
                E("vector", lambda e, sp=src, c=c_sl: e.tensor_tensor(
                    out=rtmp2[:, :], in0=sp[:, 512:1024], in1=c, op=ALU.mult),
                    inc="ve")
                ev = E("vector", lambda e, d=dst: e.tensor_add(
                    d[:, 512:1024], rtmp1[:, :], rtmp2[:, :]), inc="ve")
                dm = DMA("scalar", dram[128 * j:128 * (j + 1), :], dst[:, :],
                         waits=[ev])
                ctx[("rot_dma", id(dst))] = dm
                (qwr if sc else kwr).append(dm)
                last_ev = ev
            qk_evac[j] = last_ev
            ctx[("tbl_used", par2, s)] = last_ev
            if j + 2 < NT:
                tbl_slice(j + 2)
        wv_toks = []
        if s == 0:
            for c in range(DC):
                wv_toks.append(TDMA(wvt[:, c, :], wv_s[:, 128 * c:128 * (c + 1)],
                                    waits=[qkv_hold["v"].get("tok"), last_qk_pe]))
        else:
            wv_toks.append(DMA("scalar", wvt[:, :, :], wvTv[:, :, :],
                               waits=[rw_done["wv"], last_qk_pe]))
        v_evac = {}
        for j in range(NT):
            par2 = j % 2
            pv = ps[:, 2048 * par2:2048 * par2 + 1024]
            mmw = wv_toks + [qk_evac[NT - 2 + (j % 2)] if j < 2 else v_evac[j - 2]]
            E("tensor", lambda e, o=pv: e.matmul(
                o[:, 0:512], ones_bf[:, :], bvr[0:1, 0:512],
                start=True, stop=False), waits=mmw)
            E("tensor", lambda e, o=pv: e.matmul(
                o[:, 512:1024], ones_bf[:, :], bvr[0:1, 512:1024],
                start=True, stop=False))
            for c in range(DC):
                E("tensor", lambda e, c=c, j=j: e.ldweights(
                    xnt[:, c, 128 * j:128 * (j + 1)]))
                E("tensor", lambda e, o=pv, c=c, j=j: e.matmul(
                    o[:, 0:512], xnt[:, c, 128 * j:128 * (j + 1)], wvt[:, c, 0:512],
                    start=False, stop=False))
                E("tensor", lambda e, o=pv, c=c, j=j: e.matmul(
                    o[:, 512:1024], xnt[:, c, 128 * j:128 * (j + 1)],
                    wvt[:, c, 512:1024], start=False, stop=(c == DC - 1)),
                    inc="pe" if c == DC - 1 else None)
            mtok = ("pe", em.cnt["pe"])
            c1 = E("scalar", lambda e, o=pv, j=j: e.activation(
                out=v_res[:, j, 0:512], in_=o[:, 0:512], func=AF.Copy),
                waits=[mtok], inc="ac")
            c2 = E("scalar", lambda e, o=pv, j=j: e.activation(
                out=v_res[:, j, 512:1024], in_=o[:, 512:1024], func=AF.Copy),
                inc="ac")
            v_evac[j] = c2
        last_v_pe = mtok
        ctx["bank_guard"] = [v_evac[NT - 2], v_evac[NT - 1],
                             qk_evac[NT - 2], qk_evac[NT - 1]]
        return qwr, kwr, [v_evac[j] for j in range(NT)], last_v_pe

    def attn_phase(s, qwr, kwr, vev, last_v_pe):
        bank_guard = ctx.get("bank_guard", [])
        kT_toks = []
        for c in range(DC):
            kT_toks.append(TDMA(xnt[:, c, :], k_s[:, 128 * c:128 * (c + 1)],
                                waits=kwr + [ctx.get("xnt_free"), last_v_pe]))
        x_upd = [None] * NT
        exp_toks = {}
        stt_toks = {}
        rc_toks = {}
        qT_toks = {}
        pT_toks = {}
        pdma = {}
        sco_done = {}
        last_pe = [None]

        def emit_qT(g):
            buf = qtb if g % 2 == 0 else qtb_b
            toks = []
            for c in range(DC):
                toks.append(TDMA(buf[:, c, :],
                                 q_s[GT * g:GT * (g + 1), 128 * c:128 * (c + 1)],
                                 waits=[qwr[4 * g + l] for l in range(QPG)]
                                 + [sco_done.get(g - 2), last_v_pe, kwr[-1],
                                    ctx.get("qtb_guard")]))
            qT_toks[g] = toks

        def emit_scores(g, l):
            i = QPG * g + l
            par = i % 2
            qbuf = qtb if g % 2 == 0 else qtb_b
            mmw = qT_toks[g] + kT_toks
            if i >= 1:
                mmw.append(exp_toks[i - 1])
            else:
                mmw += bank_guard
            for c in range(DC):
                E("tensor", lambda e, c=c, l=l, qb=qbuf: e.ldweights(
                    qb[:, c, 128 * l:128 * (l + 1)]),
                    waits=mmw if c == 0 else [])
                for b in range(4):
                    E("tensor", lambda e, c=c, l=l, b=b, qb=qbuf: e.matmul(
                        ps[:, 512 * b:512 * (b + 1)],
                        qb[:, c, 128 * l:128 * (l + 1)],
                        xnt[:, c, 512 * b:512 * (b + 1)],
                        start=(c == 0), stop=(c == DC - 1)),
                        inc="pe" if (c == DC - 1 and b == 3) else None)
            mtok = ("pe", em.cnt["pe"])
            if l == QPG - 1:
                sco_done[g] = mtok
            rm = []
            for b in range(4):
                rm.append(E("vector", lambda e, b=b, p=par: e.reduce_max(
                    out=mx4[:, p, b:b + 1], in_=ps[:, 512 * b:512 * (b + 1)],
                    axis=AXX), waits=[mtok], inc="ve"))
            n1 = E("vector", lambda e, p=par, i=i: e.reduce_max(
                out=nm32[:, i:i + 1], in_=mx4[:, p, :], axis=AXX, negate=True),
                waits=rm, inc="ve")
            n2 = E("vector", lambda e, i=i: e.tensor_scalar_mul(
                nm32[:, i:i + 1], nm32[:, i:i + 1], 1.0 / 32.0), waits=[n1], inc="ve")
            exps = []
            for b in range(4):
                exps.append(E("scalar", lambda e, b=b, i=i, p=par: e.activation(
                    out=p_sb[:, 512 * b:512 * (b + 1)],
                    in_=ps[:, 512 * b:512 * (b + 1)], func=AF.Exp, scale=1.0 / 32.0,
                    bias=nm32[:, i:i + 1], accum_out=rs4[:, p, b:b + 1]),
                    waits=[n2, pdma.get(i - 1)], inc="ac"))
            exp_toks[i] = exps[-1]
            rsum = E("vector", lambda e, p=par, i=i: e.reduce_sum(
                out=rowsum[:, i:i + 1], in_=rs4[:, p, :], axis=AXX),
                waits=exps, inc="ve")
            rc_toks[i] = E("vector", lambda e, i=i: e.reciprocal(
                out=recip[:, i:i + 1], in_=rowsum[:, i:i + 1]), waits=[rsum], inc="ve")
            pdma[i] = DMA("scalar", p_s[128 * l:128 * (l + 1), :], p_sb[:, :],
                          waits=exps + [ctx.get("p_s_free")])

        def emit_pT(g):
            toks = []
            for kc in range(16):
                toks.append(TDMA(pT[:, g % 2, kc, :],
                                 p_s[:, 128 * kc:128 * (kc + 1)],
                                 waits=[pdma[QPG * g + l] for l in range(QPG)]
                                 + [stt_toks.get(QPG * (g - 2) + 3)]))
            pT_toks[g] = toks
            ctx["p_s_free"] = toks[-1]

        def emit_pv(g, l):
            i = QPG * g + l
            par = i % 2
            po = ps[:, 2048 + 1024 * par:3072 + 1024 * par]
            mmw = pT_toks[g] + [vev[-1], rc_toks[i]]
            if i >= 2:
                mmw.append(stt_toks[i - 2])
            else:
                mmw += bank_guard
            for kc in range(16):
                E("tensor", lambda e, kc=kc, l=l, g=g: e.ldweights(
                    pT[:, g % 2, kc, 128 * l:128 * (l + 1)]),
                    waits=mmw if kc == 0 else [])
                E("tensor", lambda e, o=po, kc=kc, g=g, l=l: e.matmul(
                    o[:, 0:512], pT[:, g % 2, kc, 128 * l:128 * (l + 1)],
                    v_res[:, kc, 0:512], start=(kc == 0), stop=False))
                E("tensor", lambda e, o=po, kc=kc, g=g, l=l: e.matmul(
                    o[:, 512:1024], pT[:, g % 2, kc, 128 * l:128 * (l + 1)],
                    v_res[:, kc, 512:1024], start=(kc == 0), stop=(kc == 15)),
                    inc="pe" if kc == 15 else None)
            mtok = ("pe", em.cnt["pe"])
            last_pe[0] = mtok
            E("vector", lambda e, o=po, i=i: e.scalar_tensor_tensor(
                out=x_res[:, i, 0:512], in0=o[:, 0:512], scalar=recip[:, i:i + 1],
                in1=x_res[:, i, 0:512], op0=ALU.mult, op1=ALU.add),
                waits=[mtok, rc_toks[i]], inc="ve")
            s2 = E("vector", lambda e, o=po, i=i: e.scalar_tensor_tensor(
                out=x_res[:, i, 512:1024], in0=o[:, 512:1024],
                scalar=recip[:, i:i + 1], in1=x_res[:, i, 512:1024],
                op0=ALU.mult, op1=ALU.add), inc="ve")
            stt_toks[i] = s2
            x_upd[i] = s2

        emit_qT(0)
        emit_qT(1)
        for g in range(NQG):
            for l in range(QPG):
                emit_scores(g, l)
                if g >= 2:
                    emit_pv(g - 2, l)
            emit_pT(g)
            if g + 2 < NQG:
                emit_qT(g + 2)
        for l in range(QPG):
            emit_pv(NQG - 2, l)
        for l in range(QPG):
            emit_pv(NQG - 1, l)
        ctx["bank_guard"] = [exp_toks[NT - 1], stt_toks[NT - 2], stt_toks[NT - 1]]
        ctx["hT_guard"] = last_pe[0]
        ctx["xnt_free"] = last_pe[0]
        ctx["win_free"] = last_pe[0]
        return x_upd, last_pe[0]

    ctx["ones_ready"] = E("gpsimd", lambda e: e.memset(ones_bf[:, :], 1.0), inc="gp")
    xl0 = x_load(0, "sync")
    emit_gu_prep()
    emit_wd_prep()
    emit_qk_prep("q", wq, wqp_s)
    emit_qk_prep("k", wk, wkp_s)
    emit_v_prep()
    assert prep_state["ci"] == PREP_TOTAL, prep_state["ci"]

    out_final = []
    for s in range(NSEQ):
        if s == 0:
            xl = xl0
            weave = (lambda t: drain_casts(12))
        else:
            xl = x_load(s, "gpsimd")
            weave = None
        xready = [[xl[t // 4]] for t in range(NT)]
        xn1 = ln_phase(s, xready, weave=weave)
        if s == 0:
            drain_casts()
            emit_tables(bank7_guard=[], xn_guard=xn1[NT - 1])
            emit_rewrites_gu()
        up1, guard1 = ffn_phase(s, xn1, first=(s == 0), load_wdt=(s == 0))
        if s == 0:
            ctx["ln1_done"] = xn1[NT - 1]
            emit_bias_prep()
        ctx["hTb_done"] = up1[NT - 1]

        def emit_out(upd):
            xo = x_out[s].rearrange("(t p) d -> p t d", p=128)
            for i in range(4):
                out_final.append(DMA("gpsimd", xo[:, 4 * i:4 * i + 4, :],
                                     x_res[:, 4 * i:4 * (i + 1), :],
                                     waits=[upd[4 * i + j] for j in range(4)]))

        if DBG_STAGE == "ffn1":
            emit_out(up1)
            if s == NSEQ - 1:
                out_final.append(DMA("gpsimd", h_dbg[:, 0:16, :], hT_a[:, :, :],
                                     waits=[up1[NT - 1]]))
                out_final.append(DMA("gpsimd", h_dbg[:, 16:24, :], hT_b[:, :, :],
                                     waits=[up1[NT - 1]]))
            continue
        xn2 = ln_phase(s, [[up1[t]] for t in range(NT)])
        qwr, kwr, vev, lvpe = qkv_phase(s, xn2)
        if s == 0:
            emit_rewrites_wd_qkv(up1[NT - 1])
        if DBG_STAGE == "qkv":
            if s == 0:
                for j in range(NT):
                    out_final.append(DMA("gpsimd",
                                         v_dbg[128 * j:128 * (j + 1), :],
                                         v_res[:, j, :], waits=[vev[j]]))
                out_final.append(qwr[-1])
                out_final.append(kwr[-1])
            emit_out([vev[-1]] * NT)
            ctx["xnt_free"] = lvpe
            ctx["win_free"] = lvpe
            ctx["hT_guard"] = lvpe
            break
        upA, peA = attn_phase(s, qwr, kwr, vev, lvpe)
        if DBG_STAGE == "attn":
            emit_out(upA)
            continue
        xn3 = ln_phase(s, [[upA[t]] for t in range(NT)])
        ctx["wdt_free"] = peA
        up2, _ = ffn_phase(s, xn3, first=False, load_wdt=True)
        ctx["wdt_free"] = None
        emit_out(up2)

    E("gpsimd", lambda e: e.memset(ones_f32[:, :], 1.0), waits=[out_final[-1]])

    with (
        nc.semaphore("s_pe") as s_pe,
        nc.semaphore("s_ve") as s_ve,
        nc.semaphore("s_ac") as s_ac,
        nc.semaphore("s_gp") as s_gp,
        nc.semaphore("s_dW") as s_dW,
        nc.semaphore("s_dS") as s_dS,
        nc.semaphore("s_dG") as s_dG,
        nc.semaphore("s_dP") as s_dP,
        nc.Block() as block,
    ):
        sems = dict(pe=s_pe, ve=s_ve, ac=s_ac, gp=s_gp, dW=s_dW, dS=s_dS,
                    dG=s_dG, dP=s_dP)

        def runner(stream):
            def run(e):
                for waits, build, tok in stream:
                    for sname, val in waits:
                        e.wait_ge(sems[sname], val)
                    ins = build(e)
                    if tok is not None:
                        ins.then_inc(sems[tok[0]],
                                     16 if tok[0] in ("dW", "dS", "dG", "dP") else 1)
            return run

        block.sync(runner(em.streams["sync"]))
        block.scalar(runner(em.streams["scalar"]))
        block.vector(runner(em.streams["vector"]))
        block.tensor(runner(em.streams["tensor"]))
        block.gpsimd(runner(em.streams["gpsimd"]))
    build_program.last_em = em
    return nc


_NC_CACHE = [None]


def kernel(L0_attn_x, L0_attn_wq_w, L0_attn_wq_b, L0_attn_wk_w, L0_attn_wk_b,
           L0_attn_wv_w, L0_attn_wv_b, L0_ffn_w_gate, L0_ffn_w_up, L0_ffn_w_down):
    x = np.ascontiguousarray(np.asarray(L0_attn_x, dtype=np.float32))
    wq_ = np.ascontiguousarray(np.asarray(L0_attn_wq_w, dtype=np.float32))
    wk_ = np.ascontiguousarray(np.asarray(L0_attn_wk_w, dtype=np.float32))
    wv_ = np.ascontiguousarray(np.asarray(L0_attn_wv_w, dtype=np.float32))
    bq_ = np.ascontiguousarray(np.asarray(L0_attn_wq_b, dtype=np.float32)).reshape(1, D)
    bk_ = np.ascontiguousarray(np.asarray(L0_attn_wk_b, dtype=np.float32)).reshape(1, D)
    bv_ = np.ascontiguousarray(np.asarray(L0_attn_wv_b, dtype=np.float32)).reshape(1, D)
    wg_ = np.ascontiguousarray(np.asarray(L0_ffn_w_gate, dtype=np.int32))
    wu_ = np.ascontiguousarray(np.asarray(L0_ffn_w_up, dtype=np.int32))
    wd_ = np.ascontiguousarray(np.asarray(L0_ffn_w_down, dtype=np.int32))

    if _NC_CACHE[0] is None:
        _NC_CACHE[0] = build_program()
    nc = _NC_CACHE[0]

    in_maps = []
    for core in range(8):
        in_maps.append({
            "x_in": x[2 * core:2 * core + 2],
            "wq": wq_, "wk": wk_, "wv": wv_,
            "bq": bq_, "bk": bk_, "bv": bv_,
            "wg": wg_, "wu": wu_, "wd": wd_,
        })
    res = run_bass_kernel_spmd(nc, in_maps, core_ids=list(range(8)))
    out = np.concatenate([r["x_out"] for r in res.results], axis=0)
    return np.ascontiguousarray(out.astype(np.float32))
